# version 73
# baseline (speedup 1.0000x reference)
import os
import numpy as np
import ml_dtypes
import concourse.bass as bass
import concourse.mybir as mybir
from concourse.bass_utils import run_bass_kernel_spmd
from contextlib import ExitStack

F32 = mybir.dt.float32
BF16 = mybir.dt.bfloat16
I32 = mybir.dt.int32
F32R = mybir.dt.float32r
AF = mybir.ActivationFunctionType
ALU = mybir.AluOpType

P = 128
D = 1024
DC = 8
T = 2048
NQT = 4
QTW = 512
HID = 3072
NHC = 24
GRP = 6
NH = 8
IC = 4
DH = 64
NKJ = 128
EPS = 1e-5
DEQ = 1.0 / 127.0
RG = [[0, 1, 2, 3, 4, 5, 6, 7]]

ENGS = ("te", "act", "ve", "po", "sp")
DMA_SEMS = ("ld", "ldg", "lda", "pod", "pod0", "pod1", "pod2", "cc")
QUEUE_OF = {"ld": "sp", "ldg": "sp", "lda": "act", "pod": "po",
            "pod0": "po", "pod1": "po", "pod2": "po", "cc": "po"}


class B:

    def __init__(self):
        self.q = {e: [] for e in ENGS}
        self.cnt = {e: 0 for e in ENGS}
        self.dmac = {s: 0 for s in DMA_SEMS}
        self.knows = {}
        self.last_w = {}
        self.readers = {}
        self.fence = {s: (0, None) for s in DMA_SEMS}

    def _fence_dma(self, sem):
        covered, tok = self.fence[sem]
        total = self.dmac[sem]
        if tok is not None and covered >= total:
            return tok
        qeng = QUEUE_OF[sem]
        self.q[qeng].append(([(sem, total)], "NOP", "e"))
        self.knows[(qeng, sem)] = total
        self.cnt[qeng] += 1
        tok = (qeng, self.cnt[qeng])
        self.fence[sem] = (total, tok)
        return tok

    def _emit(self, eng, fn, deps, inc):
        seen = {}
        for tok in deps:
            if tok is None:
                continue
            sem, val = tok
            if sem in DMA_SEMS:
                if QUEUE_OF[sem] == eng:
                    val = self.dmac[sem]
                else:
                    sem, val = self._fence_dma(sem)
                    if sem == eng:
                        continue
            if sem == eng == "te":
                continue
            if self.knows.get((eng, sem), 0) >= val:
                continue
            if seen.get(sem, 0) < val:
                seen[sem] = val
        waits = []
        for s, v in seen.items():
            waits.append((s, v))
            self.knows[(eng, s)] = v
        self.q[eng].append((waits, fn, inc))
        if inc == "e":
            self.cnt[eng] += 1
            return (eng, self.cnt[eng])
        elif inc == "cc":
            self.dmac["cc"] += 1
            return ("cc", self.dmac["cc"])
        elif inc in DMA_SEMS:
            self.dmac[inc] += 16
            return (inc, self.dmac[inc])
        return None

    def _deps(self, reads, writes):
        deps = []
        for key in reads:
            if key in self.last_w:
                deps.append(self.last_w[key])
        for key in writes:
            if key in self.last_w:
                deps.append(self.last_w[key])
            deps.extend(self.readers.get(key, ()))
        return deps

    def _record(self, tok, reads, writes):
        for key in reads:
            self.readers.setdefault(key, []).append(tok)
        for key in writes:
            self.last_w[key] = tok
            self.readers[key] = []
        return tok

    def op(self, eng, fn, reads=(), writes=()):
        tok = self._emit(eng, fn, self._deps(reads, writes), "e")
        return self._record(tok, reads, writes)

    def dma(self, qeng, sem, fn, reads=(), writes=()):
        tok = self._emit(qeng, fn, self._deps(reads, writes), sem)
        return self._record(tok, reads, writes)

    def barrier(self):
        for s in DMA_SEMS:
            if self.dmac[s] > 0:
                self._fence_dma(s)
        toks = [(e, self.cnt[e]) for e in ENGS]
        for eng in ENGS:
            self._emit(eng, None, [t for t in toks if t[0] != eng], None)

    def build(self, nc, ctx):
        sems = {}
        for name in ENGS + DMA_SEMS:
            sems[name] = ctx.enter_context(nc.semaphore("s_" + name))
        block = ctx.enter_context(nc.Block())

        def mk(ename):
            def run(engine):
                for waits, fn, inc in self.q[ename]:
                    for s, v in waits:
                        engine.wait_ge(sems[s], v)
                    if fn is None:
                        continue
                    if fn == "NOP":
                        engine.nop().then_inc(sems[ename], 1)
                        continue
                    inst = fn(engine)
                    if inc == "e":
                        inst.then_inc(sems[ename], 1)
                    elif inc == "cc":
                        inst.then_inc(sems["cc"], 1)
                    elif inc in DMA_SEMS:
                        inst.then_inc(sems[inc], 16)
            return run

        block.tensor(mk("te"))
        block.scalar(mk("act"))
        block.vector(mk("ve"))
        block.gpsimd(mk("po"))
        block.sync(mk("sp"))


def build_program():
    nc = bass.Bass("TRN2", target_bir_lowering=False, num_devices=8)
    b = B()

    x_in = nc.dram_tensor("x", [D, T], F32, kind="ExternalInput").ap()
    wg_in = nc.dram_tensor("wg", [NHC, D, P], I32, kind="ExternalInput").ap()
    wu_in = nc.dram_tensor("wu", [NHC, D, P], I32, kind="ExternalInput").ap()
    wd_in = nc.dram_tensor("wd", [NHC, P, D], I32, kind="ExternalInput").ap()
    wq_in = nc.dram_tensor("wq", [DC, P, 512], F32, kind="ExternalInput").ap()
    wkv_in = nc.dram_tensor("wkv", [DC, P, P], F32, kind="ExternalInput").ap()
    wo_in = nc.dram_tensor("wo", [IC, P, D], F32, kind="ExternalInput").ap()
    gatt_in = nc.dram_tensor("gatt", [P, DC], F32, kind="ExternalInput").ap()
    gout_in = nc.dram_tensor("gout", [P, DC], F32, kind="ExternalInput").ap()
    out_d = nc.dram_tensor("out", [P, DC * T], F32, kind="ExternalOutput").ap()

    cc_kin2 = [nc.dram_tensor(f"cc_kin{h}", [64, 1024], BF16, kind="Internal").ap()
               for h in range(2)]
    cc_vin2 = [nc.dram_tensor(f"cc_vin{h}", [P, 520], BF16, kind="Internal").ap()
               for h in range(2)]
    kt_ag2 = [nc.dram_tensor(f"kt_ag{h}", [8 * 64, 1024], BF16, kind="Internal").ap()
              for h in range(2)]
    v_ag2 = [nc.dram_tensor(f"v_ag{h}", [8 * P, 520], BF16, kind="Internal").ap()
             for h in range(2)]
    stg_d = nc.dram_tensor("stg_d", [16 * QTW], F32, kind="Internal").ap()

    ident_d = nc.inline_tensor(np.eye(64, dtype=ml_dtypes.bfloat16), "identc").ap()
    identf_d = nc.inline_tensor(np.eye(64, dtype=np.float32), "identfc").ap()
    onesbf_d = nc.inline_tensor(np.ones((P, 1), dtype=ml_dtypes.bfloat16), "onesbfc").ap()
    consts_d = nc.inline_tensor(
        np.tile(np.array([[-16.0, EPS, DEQ, 16.0, 0.5, 1.0 / D]],
                         dtype=np.float32), (P, 1)),
        "constsc",
    ).ap()
    sel2_np = np.zeros((P, 2), dtype=ml_dtypes.bfloat16)
    sel2_np[0:64, 0] = 1.0
    sel2_np[64:128, 1] = 1.0
    sel2_d = nc.inline_tensor(sel2_np, "sel2c").ap()
    ones64_d = nc.inline_tensor(np.ones((64, 1), dtype=ml_dtypes.bfloat16), "ones64c").ap()
    e36_np = np.zeros((36, 8 * P), dtype=np.float32)
    for qt in range(4):
        e36_np[qt, qt * P:(qt + 1) * P] = 1.0
        e36_np[32 + qt, (4 + qt) * P:(5 + qt) * P] = 1.0
    e36_d = nc.inline_tensor(e36_np, "e36c").ap()
    e2h_np = np.zeros((8, IC * P), dtype=np.float32)
    for ic in range(IC):
        e2h_np[2 * ic, ic * P:ic * P + 64] = 1.0
        e2h_np[2 * ic + 1, ic * P + 64:(ic + 1) * P] = 1.0
    e2h_d = nc.inline_tensor(e2h_np, "e2hc").ap()
    esel_np = np.zeros((1, 96), dtype=np.float32)
    for r in range(4):
        esel_np[0, r * 4 + r] = 1.0
    for r in range(8):
        esel_np[0, 16 + r * 8 + r] = 1.0
    for r in range(2):
        esel_np[0, 80 + r * 2 + r] = 1.0
    esel_d = nc.inline_tensor(esel_np, "eselc").ap()

    ctx = ExitStack()
    sb = lambda name, shape, dt: ctx.enter_context(nc.sbuf_tensor(name, shape, dt))

    xT = sb("xT", [P, DC, T], F32)
    consts = sb("consts", [P, 6], F32)
    ident = sb("ident", [64, 64], BF16)
    identf = sb("identf", [64, 64], F32)
    onesbf = sb("onesbf", [P, 1], BF16)
    ones64 = sb("ones64", [64, 1], BF16)
    sel2 = sb("sel2", [P, 2], BF16)
    e36 = sb("e36", [36, 8 * P], F32R)
    e2h = sb("e2h", [8, IC * P], F32R)
    g_att = sb("g_att", [P, DC], F32)
    g_out = sb("g_out", [P, DC], F32)
    stage = sb("stage", [1, 8 * QTW], F32R)
    coefx = sb("coefx", [36, QTW], F32R)
    crS = sb("crS", [4, QTW], F32)
    crQ = sb("crQ", [4, QTW], F32)
    crA = sb("crA", [4, QTW], F32)
    crAr = sb("crAr", [4, QTW], F32R)
    crB = (sb("crB", [1, QTW], F32)
           if os.environ.get("MK_PSTATS", "0") == "1" else None)
    t1b = sb("t1b", [P, QTW], BF16)
    bsb = sb("bsb", [P, 2, QTW], BF16)
    wkv_b = sb("wkv_b", [P, DC, P], BF16)
    esel = sb("esel", [1, 96], F32R)
    gatt_bf = sb("gatt_bf", [P, DC], BF16)

    PSUM = ctx.enter_context(nc.psum_tensor("PSUM", [P, 4096], F32))

    def bank(k, nbanks=1, rows=P):
        return PSUM[0:rows, k * QTW:(k + nbanks) * QTW]

    def bank_bf16(k, rows=P, cols=1024):
        return PSUM.ap().bitcast(BF16)[0:rows, k * 1024:k * 1024 + cols]

    pkeys = lambda k, n=1: tuple(f"ps{j}" for j in range(k, k + n))

    def _lp(fn):
        def g(e):
            with nc.allow_low_precision("f32r rounding of broadcast operands"):
                return fn(e)
        return g

    _oneshot_gather = os.environ.get("MK_G1", "0") == "1"

    def emit_gather(nslots, dst, dst_key, col0=0, mm=False, pb=7):
        if mm:
            off = {4: 0, 8: 16, 1: 16, 2: 80}[nslots]
            for r in range(nslots):
                b.op("te", (lambda r=r, n=nslots, c0=col0, off=off, pb=pb: lambda e: e.matmul(
                    bank(pb, rows=n), esel[0:1, off + r * n:off + (r + 1) * n],
                    stage[0:1, (c0 + r) * QTW:(c0 + r + 1) * QTW],
                    start=(r == 0), stop=(r == n - 1)))(),
                    reads=("stage", "esel"), writes=pkeys(pb))
            b.op("ve", (lambda n=nslots, dst=dst, pb=pb: lambda e: e.tensor_copy(
                dst, bank(pb, rows=n)))(),
                reads=pkeys(pb), writes=(dst_key,))
            return
        if _oneshot_gather:
            b.dma("sp", "ldg", (lambda n=nslots, c0=col0, dst=dst: lambda e: e.dma_start(
                out=dst,
                in_=stage.ap().bitcast(F32)[0:1, c0 * QTW:(c0 + n) * QTW].rearrange(
                    "o (r n) -> (o r) n", n=QTW)))(),
                reads=("stage",), writes=(dst_key,))
            return
        b.dma("sp", "ldg", (lambda n=nslots, c0=col0: lambda e: e.dma_start(
            out=stg_d[0:n * QTW],
            in_=stage.ap().bitcast(F32)[0:1, c0 * QTW:(c0 + n) * QTW]))(),
            reads=("stage",), writes=("stg_d",))
        b.dma("sp", "ldg", (lambda n=nslots, dst=dst: lambda e: e.dma_start(
            out=dst, in_=stg_d[0:n * QTW].rearrange("(r n) -> r n", n=QTW)))(),
            reads=("stg_d",), writes=(dst_key,))

    _xq = [("sp", "ld"), ("sp", "ldg"), ("po", "pod2"), ("sp", "ld")]
    for qt in range(NQT):
        tsl = slice(qt * QTW, (qt + 1) * QTW)
        qe, sm = _xq[qt]
        b.dma(qe, sm, (lambda tsl=tsl, qt=qt: lambda e: e.dma_start(
            out=xT[:, :, tsl],
            in_=x_in.rearrange("(dc p) t -> p dc t", p=P)[:, :, tsl]))(),
            writes=(f"xT{qt}",))
    _pending_selectors = []
    for dst, src, key in (
        (consts, consts_d, "consts"), (ident, ident_d, "ident"),
        (identf, identf_d, "identf"),
        (onesbf, onesbf_d, "onesbf"), (ones64, ones64_d, "ones64"),
        (sel2, sel2_d, "sel2"), (e36, e36_d, "e36"),
        (e2h, e2h_d, "e2h"), (esel, esel_d, "esel"),
        (g_att, gatt_in, "g_att"), (g_out, gout_in, "g_out"),
    ):
        if dst.dtype == F32R:
            _pending_selectors.append((dst, src, key))
        else:
            b.dma("sp", "ld", (lambda d, s: lambda e: e.dma_start(
                out=d[:], in_=s))(dst, src),
                writes=(key,))

    def load_wkv():
        b.dma("po", "pod1", lambda e: e.dma_start(
            out=wkv_b[:], in_=wkv_in.rearrange("dc p w -> p dc w")),
            writes=("wkv_b",))
    b.op("ve", lambda e: e.tensor_copy(gatt_bf[:], g_att[:]),
         reads=("g_att",), writes=("gatt_bf",))

    _ffn_xb = [None]



    def emit_stats_qt(qt, xb_sc, pb=(6, 7)):
        tsl = slice(qt * QTW, (qt + 1) * QTW)
        for dc in range(DC):
            bf = dc % 2
            b.op("ve", (lambda bf=bf, dc=dc, tsl=tsl, qt=qt: lambda e: e.tensor_copy(
                xb_sc[:, 0, bf, :], xT[:, dc, tsl]))(),
                reads=(f"xT{qt}",), writes=(f"xb{bf}",))
            b.op("act", (lambda bf=bf, xb_sc=xb_sc: lambda e: e.activation(
                xb_sc[:, 1, bf, :], xb_sc[:, 0, bf, :], AF.Square))(),
                reads=(f"xb{bf}",), writes=(f"xq{bf}",))
            b.op("te", (lambda bf=bf, dc=dc, xb_sc=xb_sc, pb=pb: lambda e: e.matmul(
                bank(pb[0], rows=1), onesbf[:], xb_sc[:, 0, bf, :],
                start=(dc == 0), stop=(dc == DC - 1)))(),
                reads=(f"xb{bf}", "onesbf"), writes=pkeys(pb[0]))
            b.op("te", (lambda bf=bf, dc=dc, xb_sc=xb_sc, pb=pb: lambda e: e.matmul(
                bank(pb[1], rows=1), onesbf[:], xb_sc[:, 1, bf, :],
                start=(dc == 0), stop=(dc == DC - 1)))(),
                reads=(f"xq{bf}", "onesbf"), writes=pkeys(pb[1]))
        b.op("ve", (lambda qt=qt, pb=pb: _lp(lambda e: e.tensor_copy(
            stage[0:1, qt * QTW:(qt + 1) * QTW], bank(pb[0], rows=1))))(),
            reads=pkeys(pb[0]), writes=("stage",))
        b.op("ve", (lambda qt=qt, pb=pb: _lp(lambda e: e.tensor_copy(
            stage[0:1, (4 + qt) * QTW:(5 + qt) * QTW], bank(pb[1], rows=1))))(),
            reads=pkeys(pb[1]), writes=("stage",))

    def emit_coef_block(attn_double):
        emit_gather(4, crS[0:4, :], "crS", col0=0, mm=True, pb=6)
        emit_gather(4, crQ[0:4, :], "crQ", col0=4, mm=True, pb=7)
        _emit_coef_math(attn_double)

    def emit_ln_coefs(tag, attn_double, xb_sc=None, stats_mode="compute"):
        if stats_mode == "done":
            return
        if stats_mode == "dram":
            b.dma("sp", "ldg", lambda e: e.dma_start(
                out=crS[0:4, :],
                in_=stg_d[8 * QTW:12 * QTW].rearrange("(r n) -> r n", n=QTW)),
                reads=("stg_dS",), writes=("crS",))
            b.dma("sp", "ldg", lambda e: e.dma_start(
                out=crQ[0:4, :],
                in_=stg_d[12 * QTW:16 * QTW].rearrange("(r n) -> r n", n=QTW)),
                reads=("stg_dQ",), writes=("crQ",))
        else:
            if stats_mode == "compute":
                for qt in range(NQT):
                    emit_stats_qt(qt, xb_sc)
            emit_gather(4, crS[0:4, :], "crS", col0=0, mm=True, pb=6)
            emit_gather(4, crQ[0:4, :], "crQ", col0=4, mm=True, pb=7)
        _emit_coef_math(attn_double)

    def _emit_coef_math(attn_double):
        b.op("ve", lambda e: e.tensor_scalar_mul(crS[:], crS[:], 1.0 / D),
             reads=("crS",), writes=("crS",))
        b.op("ve", lambda e: e.tensor_scalar_mul(crQ[:], crQ[:], 1.0 / D),
             reads=("crQ",), writes=("crQ",))
        b.op("ve", lambda e: e.tensor_mul(crA[:], crS[:], crS[:]),
             reads=("crS",), writes=("crA",))
        b.op("ve", lambda e: e.tensor_sub(crQ[:], crQ[:], crA[:]),
             reads=("crA", "crQ"), writes=("crQ",))
        if attn_double:
            b.op("ve", lambda e: e.tensor_scalar(
                crA[:], crQ[:], 1.0 + EPS, EPS * EPS, ALU.mult, ALU.add),
                reads=("crQ",), writes=("crA",))
            b.op("act", lambda e: e.activation(crA[:], crA[:], AF.Sqrt),
                 reads=("crA",), writes=("crA",))
        else:
            b.op("act", lambda e: e.activation(
                crA[:], crQ[:], AF.Sqrt, bias=consts[0:4, 1:2]),
                reads=("crQ", "consts"), writes=("crA",))
        b.op("ve", lambda e: e.reciprocal(crA[:], crA[:]),
             reads=("crA",), writes=("crA",))
        b.op("ve", lambda e: e.tensor_mul(crS[:], crS[:], crA[:]),
             reads=("crA", "crS"), writes=("crS",))
        b.op("ve", _lp(lambda e: e.tensor_copy(coefx[32:36, :], crA[:])),
             reads=("crA",), writes=("coefx",))
        b.op("ve", _lp(lambda e: e.tensor_copy(coefx[0:4, :], crS[:])),
             reads=("crS",), writes=("coefx",))

    def emit_ln_apply(xnT, gain, after_qt=None):
        assert gain is None
        for qt in range(NQT):
            tsl = slice(qt * QTW, (qt + 1) * QTW)
            pa, pb_ = (6, 7) if qt % 2 == 0 else (4, 5)
            par = qt % 2
            b.op("te", (lambda qt=qt, pa=pa: lambda e: e.matmul(
                bank(pa), e36[:, (4 + qt) * P:(5 + qt) * P], coefx[0:36, :],
                start=True, stop=True))(),
                reads=("coefx", "e36"), writes=pkeys(pa))
            b.op("te", (lambda qt=qt, pb_=pb_: lambda e: e.matmul(
                bank(pb_), e36[:, qt * P:(qt + 1) * P], coefx[0:36, :],
                start=True, stop=True))(),
                reads=("coefx", "e36"), writes=pkeys(pb_))
            b.op("act", (lambda pb_=pb_, par=par: lambda e: e.activation(
                bsb[:, par, :], bank(pb_), AF.Copy))(),
                reads=pkeys(pb_), writes=(f"bsb{par}",))
            if after_qt is not None and qt > 0:
                after_qt(qt - 1)
            for dc in range(DC):
                b.op("ve", (lambda dc=dc, tsl=tsl, pa=pa: lambda e: e.tensor_mul(
                    t1b[:], xT[:, dc, tsl], bank(pa)))(),
                    reads=(f"xT{qt}",) + pkeys(pa), writes=("t1",))
                b.op("ve", (lambda dc=dc, tsl=tsl, par=par: lambda e: e.tensor_sub(
                    xnT[:, dc, tsl], t1b[:], bsb[:, par, :]))(),
                    reads=("t1", f"bsb{par}"), writes=(f"xnT{qt}",))
        if after_qt is not None:
            after_qt(NQT - 1)

    def emit_ffn(pass_id, post_hook=None, skip_stats=False):
        with ExitStack() as fctx:
            fsb = lambda name, shape, dt: fctx.enter_context(
                nc.sbuf_tensor(f"{name}_{pass_id}", shape, dt))
            xnT = fsb("xnT", [P, DC, T], BF16)
            xb_f = fsb("xb_f", [P, 2, 2, QTW], BF16)
            _ffn_xb[0] = xb_f
            if pass_id == 0:
                selstg_t = fsb("selstg", [36, 8 * P], F32)
                selstg = selstg_t.ap()
                for dst, srcap, key in _pending_selectors:
                    nr, ncol = dst.shape
                    b.dma("sp", "ld", (lambda s=srcap, nr=nr, ncol=ncol: lambda e: e.dma_start(
                        out=selstg[0:nr, 0:ncol], in_=s))(),
                        reads=("xb",), writes=("xb",))
                    b.op("ve", (lambda d=dst, nr=nr, ncol=ncol: _lp(lambda e: e.tensor_copy(
                        d[:], selstg[0:nr, 0:ncol])))(),
                        reads=("xb",), writes=(key, "xb"))
                b.op("ve", lambda e: e.memset(selstg[0:36, 0:QTW], 0.0),
                     reads=("xb",), writes=("xb",))
                b.op("ve", _lp(lambda e: e.tensor_copy(coefx[:], selstg[0:36, 0:QTW])),
                     reads=("xb",), writes=("coefx", "xb"))
            wgb = fsb("wgb", [P, 3, D], BF16)
            wub = fsb("wub", [P, 3, D], BF16)
            wdb = fsb("wdb", [P, GRP + 2, D], BF16)
            hT = fsb("hT", [P, GRP, T], BF16)
            sg = fsb("sg", [P, 2, QTW], BF16)
            su = fsb("su", [P, 2, QTW], BF16)

            def load_wg(hc, buf):
                b.dma("po", f"pod{hc % 3}", (lambda hc=hc, buf=buf: lambda e: e.dma_start(
                    out=wgb[:, buf, :].rearrange("p (dc h) -> p dc h", h=P),
                    in_=wg_in[hc].rearrange("(dc p) h -> p dc h", p=P)))(),
                    writes=(f"wgb{buf}",))

            def load_wu(hc, buf):
                b.dma("po", f"pod{hc % 3}", (lambda hc=hc, buf=buf: lambda e: e.dma_start(
                    out=wub[:, buf, :].rearrange("p (dc h) -> p dc h", h=P),
                    in_=wu_in[hc].rearrange("(dc p) h -> p dc h", p=P)))(),
                    writes=(f"wub{buf}",))

            def load_gu(hc, buf):
                load_wg(hc, buf)
                load_wu(hc, buf)

            def load_wd(hc):
                b.dma("po", f"pod{hc % 3}", (lambda hc=hc: lambda e: e.dma_start(
                    out=wdb[:, hc % (GRP + 2), :], in_=wd_in[hc]))(),
                    writes=(f"wdb{hc % (GRP + 2)}",))

            load_gu(0, 0)
            load_wd(0)

            def gu_mms(j, buf, qt):
                tsl = slice(qt * QTW, (qt + 1) * QTW)
                pg, pu = (0, 1) if (qt % 2 == 0) else (2, 3)
                for dc in range(DC):
                    b.op("te", (lambda dc=dc, buf=buf, tsl=tsl, pg=pg, qt=qt: lambda e: e.matmul(
                        bank(pg), wgb[:, buf, dc * P:(dc + 1) * P],
                        xnT[:, dc, tsl],
                        start=(dc == 0), stop=(dc == DC - 1)))(),
                        reads=(f"wgb{buf}", f"xnT{qt}"), writes=pkeys(pg))
                for dc in range(DC):
                    b.op("te", (lambda dc=dc, buf=buf, tsl=tsl, pu=pu, qt=qt: lambda e: e.matmul(
                        bank(pu), wub[:, buf, dc * P:(dc + 1) * P],
                        xnT[:, dc, tsl],
                        start=(dc == 0), stop=(dc == DC - 1)))(),
                        reads=(f"wub{buf}", f"xnT{qt}"), writes=pkeys(pu))

            def gu_evac(j, qt):
                tsl = slice(qt * QTW, (qt + 1) * QTW)
                pg, pu = (0, 1) if (qt % 2 == 0) else (2, 3)
                sb_par = qt % 2
                b.op("act", (lambda pg=pg, sb_par=sb_par: lambda e: e.activation(
                    sg[:, sb_par, :], bank(pg), AF.Silu, scale=consts[:, 2:3]))(),
                    reads=pkeys(pg) + ("consts",), writes=(f"sg{sb_par}",))
                b.op("act", (lambda pu=pu, sb_par=sb_par: lambda e: e.activation(
                    su[:, sb_par, :], bank(pu), AF.Copy, scale=consts[:, 2:3]))(),
                    reads=pkeys(pu) + ("consts",), writes=(f"su{sb_par}",))
                b.op("ve", (lambda j=j, tsl=tsl, sb_par=sb_par: lambda e: e.tensor_mul(
                    hT[:, j, tsl], sg[:, sb_par, :], su[:, sb_par, :]))(),
                    reads=(f"sg{sb_par}", f"su{sb_par}"), writes=(f"hT{j}",))

            def gu_qt(j, buf, qt):
                gu_mms(j, buf, qt)
                gu_evac(j, qt)

            _h0_hoisted = os.environ.get("MK_NO_LN", "0") != "1"
            if os.environ.get("MK_NO_LN", "0") == "1":
                b.op("ve", lambda e: e.memset(
                    xnT.ap().rearrange("p a b -> p (a b)"), 0.01),
                    writes=("xnT0", "xnT1", "xnT2", "xnT3"))
            else:
                emit_ln_coefs(f"ffn{pass_id}", attn_double=False, xb_sc=xb_f,
                              stats_mode=("dram" if skip_stats else "compute"))
                emit_ln_apply(xnT, gain=None,
                              after_qt=lambda qt: (gu_mms(0, 0, qt)
                                                   if qt < 2 else None))

            for g in range(NHC // GRP):
                for j in range(GRP):
                    hc = g * GRP + j
                    buf = hc % 3
                    if pass_id == 0 and hc == GRP:
                        load_wkv()
                    for qt in range(NQT):
                        if hc == 0 and qt == 3:
                            load_gu(1, 1)
                            load_wd(1)
                        if hc + 2 < NHC:
                            if qt == 0:
                                load_wg(hc + 2, (hc + 2) % 3)
                            elif qt == 1:
                                load_wu(hc + 2, (hc + 2) % 3)
                            elif qt == 2:
                                load_wd(hc + 2)
                        if hc == 0 and _h0_hoisted and qt < 2:
                            gu_evac(j, qt)
                        else:
                            gu_qt(j, buf, qt)
                last_group = (g == NHC // GRP - 1)
                for qt in range(NQT):
                    tsl = slice(qt * QTW, (qt + 1) * QTW)
                    for dc in range(DC):
                        py = 4 + (dc % 2)
                        for j in range(GRP):
                            hcj = g * GRP + j
                            slot = hcj % (GRP + 2)
                            b.op("te", (lambda j=j, slot=slot, dc=dc, tsl=tsl, py=py: lambda e: e.matmul(
                                bank(py), wdb[:, slot, dc * P:(dc + 1) * P],
                                hT[:, j, tsl],
                                start=(j == 0), stop=(j == GRP - 1)))(),
                                reads=(f"wdb{slot}", f"hT{j}"), writes=pkeys(py))
                        b.op("ve", (lambda dc=dc, tsl=tsl, py=py: lambda e: e.scalar_tensor_tensor(
                            xT[:, dc, tsl], bank(py), 0.5 * DEQ, xT[:, dc, tsl],
                            ALU.mult, ALU.add))(),
                            reads=pkeys(py) + (f"xT{qt}",), writes=(f"xT{qt}",))
                    if last_group and dc == DC - 1 and post_hook is not None:
                        post_hook(qt)

        b.barrier()

    def emit_attn(sub=3, stats_done=False, emit_post_stats=False):
        actx = ExitStack()
        asb = lambda name, shape, dt: actx.enter_context(nc.sbuf_tensor(name, shape, dt))
        qT = asb("qT", [P, IC, T], BF16)
        wo_b = asb("wo_b", [P, IC, D], BF16)
        pctx = ExitStack()
        psb = lambda name, shape, dt: pctx.enter_context(nc.sbuf_tensor(name, shape, dt))
        xnT = psb("xnT_a", [P, DC, T], BF16)
        xb_a = None if stats_done else psb("xb_a", [P, 2, 2, QTW], BF16)
        kq = psb("kq", [9, QTW], F32R)
        kqw = psb("kqw", [8, QTW], F32)
        q16 = psb("q16", [P, 2, IC, QTW], BF16)
        scr8k = psb("scr8k", [P, T], F32)
        kvf = scr8k.ap().rearrange("p (a b) -> p a b", a=NQT)
        wstg = scr8k.ap().rearrange("p (a b) -> p a b", a=2)
        ktown = psb("ktown", [64, T], BF16)
        vtb = psb("vtb", [64, 2, P], F32)
        vtok = psb("vtok", [P, 16, 65], BF16)
        qsq = psb("qsq", [P, IC, QTW], BF16)
        ksq = psb("ksq", [64, QTW], BF16)
        wq_b = psb("wq_b", [P, DC, 512], BF16)

        prep_lvl = int(os.environ.get("MK_PREP", "9"))

        def load_wq():
            b.dma("po", "pod0", lambda e: e.dma_start(
                out=wq_b[:], in_=wq_in.rearrange("dc p w -> p dc w")),
                writes=("wq_b",))
        b.op("ve", lambda e: e.memset(vtok.ap().rearrange("p a b -> p (a b)")[:, :], 0.0),
             writes=("vtok",))
        for t in range(16):
            b.op("ve", (lambda t=t: lambda e: e.memset(vtok[:, t, 64:65], 1.0))(),
                 reads=(), writes=("vtok",))

        emit_ln_coefs("attn", attn_double=True, xb_sc=xb_a,
                      stats_mode=("done" if stats_done else "compute"))

        m_bf = psb("m_bf", [1, T], BF16)
        aTc_sb = psb("aTc_sb", [P, 16], F32)
        cqn_bf = psb("cqn_bf", [1, IC * P + P], BF16)
        b.op("act", lambda e: e.activation(
            m_bf[0:1, :], stage[0:1, 0:NQT * QTW], AF.Copy,
            scale=consts[0:1, 5:6]),
            reads=("stage", "consts"), writes=("m_bf",))
        for dc in range(DC):
            b.op("te", (lambda dc=dc: lambda e: e.matmul(
                PSUM[0:1, 5 * QTW:5 * QTW + P],
                gatt_bf[:, dc:dc + 1], wkv_b[:, dc, :],
                start=(dc == 0), stop=(dc == DC - 1)))(),
                reads=("gatt_bf", "wkv_b"), writes=pkeys(5))
        b.op("ve", lambda e: e.tensor_scalar_mul(
            cqn_bf[0:1, IC * P:IC * P + P], PSUM[0:1, 5 * QTW:5 * QTW + P], -1.0),
            reads=pkeys(5), writes=("cqn",))

        def emit_cq():
            for ic in range(IC):
                for dc in range(DC):
                    b.op("te", (lambda ic=ic, dc=dc: lambda e: e.matmul(
                        PSUM[0:1, 4 * QTW + ic * P:4 * QTW + (ic + 1) * P],
                        gatt_bf[:, dc:dc + 1], wq_b[:, dc, ic * P:(ic + 1) * P],
                        start=(dc == 0), stop=(dc == DC - 1)))(),
                        reads=("gatt_bf", "wq_b"), writes=pkeys(4))
            b.op("ve", lambda e: e.tensor_scalar_mul(
                cqn_bf[0:1, 0:IC * P], PSUM[0:1, 4 * QTW:4 * QTW + IC * P], -1.0),
                reads=pkeys(4), writes=("cqn",))
        for tc in range(4):
            b.op("te", (lambda tc=tc: lambda e: e.transpose(
                PSUM[0:P, 5 * QTW + 256 + tc * 4:5 * QTW + 256 + (tc + 1) * 4],
                crA[0:4, tc * P:(tc + 1) * P], identf[0:4, 0:4]))(),
                reads=("crA", "identf"), writes=pkeys(5))
        b.op("ve", lambda e: e.tensor_copy(
            aTc_sb[:], PSUM[0:P, 5 * QTW + 256:5 * QTW + 272]),
            reads=pkeys(5), writes=("aTc",))

        def kv_mms(qt):
            tsl = slice(qt * QTW, (qt + 1) * QTW)
            pk = qt % 2
            for dc in range(DC):
                b.op("te", (lambda dc=dc, tsl=tsl, pk=pk, qt=qt: lambda e: e.matmul(
                    bank(pk), wkv_b[:, dc, :], xnT[:, dc, tsl],
                    start=(dc == 0), stop=False))(),
                    reads=("wkv_b", f"xnTd{qt}_{dc}"), writes=pkeys(pk))
            b.op("te", (lambda tsl=tsl, pk=pk: lambda e: e.matmul(
                bank(pk), cqn_bf[0:1, IC * P:IC * P + P], m_bf[0:1, tsl],
                start=False, stop=True))(),
                reads=("cqn", "m_bf"), writes=pkeys(pk))

        def kv_post(qt):
            tsl = slice(qt * QTW, (qt + 1) * QTW)
            pk = qt % 2
            b.op("act", (lambda qt=qt, pk=pk: lambda e: e.activation(
                kvf[:, qt, :], bank(pk), AF.Copy))(),
                reads=pkeys(pk), writes=("scr8k",))
            b.op("act", (lambda qt=qt: lambda e: e.activation(
                ksq[:], kvf[0:64, qt, :], AF.Square))(),
                reads=("scr8k",), writes=("ksq",))
            b.op("te", lambda e: e.matmul(bank(6, rows=1), ones64[:], ksq[:],
                                          start=True, stop=True),
                 reads=("ksq", "ones64"), writes=pkeys(6))
            b.op("ve", (lambda qt=qt: _lp(lambda e: e.tensor_copy(
                stage[0:1, qt * QTW:(qt + 1) * QTW], bank(6, rows=1))))(),
                reads=pkeys(6), writes=("stage",))
            for tc in range(4):
                csl = slice(tc * P, (tc + 1) * P)
                pv_ = 2 + (tc % 2)
                vb = tc % 2
                b.op("ve", (lambda qt=qt, csl=csl, vb=vb: lambda e: e.tensor_copy(
                    vtb[:, vb, :], kvf[64:128, qt, csl]))(),
                    reads=("scr8k",), writes=(f"vtb{vb}",))
                b.op("te", (lambda pv_=pv_, vb=vb: lambda e: e.transpose(
                    bank(pv_)[0:P, 0:64], vtb[:, vb, :], identf[:]))(),
                    reads=(f"vtb{vb}", "identf"), writes=pkeys(pv_))
                tile = qt * 4 + tc
                acol = (tile % 4) * 4 + tile // 4
                b.op("act", (lambda tile=tile, pv_=pv_, acol=acol: lambda e: e.activation(
                    vtok[:, tile, 0:64], bank(pv_)[0:P, 0:64], AF.Copy,
                    scale=aTc_sb[:, acol:acol + 1]))(),
                    reads=pkeys(pv_) + ("aTc",), writes=("vtok",))

        def k_half(h):
            b.dma("sp", "ld", (lambda h=h: lambda e: e.dma_start(
                out=cc_vin2[h],
                in_=vtok[:, 8 * h:8 * (h + 1), :].rearrange("p a b -> p (a b)")))(),
                reads=("vtok",), writes=(f"cc_vin{h}",))
            b.dma("po", "cc", (lambda h=h: lambda e: e.collective_compute(
                "AllGather", ALU.bypass, ins=[cc_vin2[h].opt()],
                outs=[v_ag2[h].opt()], replica_groups=RG))(),
                reads=(f"cc_vin{h}",), writes=(f"v_ag{h}",))
            emit_gather(2, crA[0:2, :], "crA", col0=2 * h, mm=True, pb=7)
            b.op("ve", lambda e: e.tensor_scalar_max(crA[0:2, :], crA[0:2, :], 1e-24),
                 reads=("crA",), writes=("crA",))
            b.op("act", lambda e: e.activation(crA[0:2, :], crA[0:2, :], AF.Sqrt),
                 reads=("crA",), writes=("crA",))
            b.op("ve", lambda e: e.tensor_scalar_mul(crA[0:2, :], crA[0:2, :], 0.25),
                 reads=("crA",), writes=("crA",))
            b.op("ve", _lp(lambda e: e.reciprocal(crAr[0:2, :], crA[0:2, :])),
                 reads=("crA",), writes=("crAr",))
            for qt in (2 * h, 2 * h + 1):
                tsl = slice(qt * QTW, (qt + 1) * QTW)
                pb = 6 + (qt % 2)
                b.op("te", (lambda qt=qt, pb=pb: lambda e: e.matmul(
                    bank(pb), e36[0:2, (qt % 2) * P:(qt % 2 + 1) * P],
                    crAr[0:2, :], start=True, stop=True))(),
                    reads=("crAr", "e36"), writes=pkeys(pb))
                b.op("ve", (lambda qt=qt, tsl=tsl, pb=pb: lambda e: e.tensor_mul(
                    ktown[:, tsl], kvf[0:64, qt, :], bank(pb)[0:64, :]))(),
                    reads=("scr8k",) + pkeys(pb), writes=("ktown",))
            b.dma("sp", "ld", (lambda h=h: lambda e: e.dma_start(
                out=cc_kin2[h], in_=ktown[:, h * 1024:(h + 1) * 1024]))(),
                reads=("ktown",), writes=(f"cc_kin{h}",))
            b.dma("po", "cc", (lambda h=h: lambda e: e.collective_compute(
                "AllGather", ALU.bypass, ins=[cc_kin2[h].opt()],
                outs=[kt_ag2[h].opt()], replica_groups=RG))(),
                reads=(f"cc_kin{h}",), writes=(f"kt_ag{h}",))


        for qt in range(NQT):
            tsl = slice(qt * QTW, (qt + 1) * QTW)
            for dc in range(DC):
                if dc % 2 == 0:
                    b.op("act", (lambda dc=dc, tsl=tsl, qt=qt: lambda e: e.activation(
                        xnT[:, dc, tsl], xT[:, dc, tsl], AF.Copy,
                        scale=g_att[:, dc:dc + 1]))(),
                        reads=(f"xT{qt}", "g_att"),
                        writes=(f"xnT{qt}", f"xnTd{qt}_{dc}"))
                else:
                    b.op("ve", (lambda dc=dc, tsl=tsl, qt=qt: lambda e: e.tensor_scalar_mul(
                        xnT[:, dc, tsl], xT[:, dc, tsl], g_att[:, dc:dc + 1]))(),
                        reads=(f"xT{qt}", "g_att"),
                        writes=(f"xnT{qt}", f"xnTd{qt}_{dc}"))
            kv_mms(qt)
            if qt > 0:
                kv_post(qt - 1)
            if qt == 2:
                k_half(0)
                load_wq()
        kv_post(NQT - 1)
        k_half(1)
        b.dma("po", "pod2", lambda e: e.dma_start(
            out=wo_b[:], in_=wo_in.rearrange("ic p w -> p ic w")),
            writes=("wo_b",))

        emit_cq()

        def q_coef_math(qt):
            emit_gather(8, kqw[0:8, :], "kq", col0=0, mm=True, pb=7)
            b.op("ve", lambda e: e.tensor_scalar_max(kqw[0:8, :], kqw[0:8, :], 1e-24),
                 reads=("kq",), writes=("kq",))
            b.op("act", lambda e: e.activation(kqw[0:8, :], kqw[0:8, :], AF.Sqrt),
                 reads=("kq",), writes=("kq",))
            b.op("ve", lambda e: e.tensor_scalar_mul(kqw[0:8, :], kqw[0:8, :], 0.25),
                 reads=("kq",), writes=("kq",))
            b.op("ve", _lp(lambda e: e.reciprocal(kq[0:8, :], kqw[0:8, :])),
                 reads=("kq",), writes=("kq",))

        def q_coef_bcast(qt):
            tsl = slice(qt * QTW, (qt + 1) * QTW)
            qp = qt % 2
            for ic in range(IC):
                pb = 6 + (ic % 2)
                b.op("te", (lambda ic=ic, pb=pb: lambda e: e.matmul(
                    bank(pb), e2h[:, ic * P:(ic + 1) * P], kq[0:8, :],
                    start=True, stop=True))(),
                    reads=("kq", "e2h"), writes=pkeys(pb))
                b.op("ve", (lambda ic=ic, tsl=tsl, pb=pb, qp=qp: lambda e: e.tensor_mul(
                    qT[:, ic, tsl], q16[:, qp, ic, :], bank(pb)))(),
                    reads=(f"q16_{qp}_{ic}",) + pkeys(pb), writes=("qT",))

        for qt in range(NQT):
            tsl = slice(qt * QTW, (qt + 1) * QTW)
            qp = qt % 2
            for ic in range(IC):
                pq = ic
                for dc in range(DC):
                    b.op("te", (lambda dc=dc, ic=ic, tsl=tsl, pq=pq, qt=qt: lambda e: e.matmul(
                        bank(pq), wq_b[:, dc, ic * P:(ic + 1) * P], xnT[:, dc, tsl],
                        start=(dc == 0), stop=False))(),
                        reads=("wq_b", f"xnTd{qt}_{dc}"), writes=pkeys(pq))
                b.op("te", (lambda ic=ic, tsl=tsl, pq=pq: lambda e: e.matmul(
                    bank(pq), cqn_bf[0:1, ic * P:(ic + 1) * P], m_bf[0:1, tsl],
                    start=False, stop=True))(),
                    reads=("cqn", "m_bf"), writes=pkeys(pq))
                b.op("act", (lambda ic=ic, pq=pq, qp=qp: lambda e: e.activation(
                    q16[:, qp, ic, :], bank(pq), AF.Copy,
                    scale=consts[:, 3:4]))(),
                    reads=pkeys(pq) + ("consts",), writes=(f"q16_{qp}_{ic}",))
                b.op("act", (lambda ic=ic, qp=qp: lambda e: e.activation(
                    qsq[:, ic, :], q16[:, qp, ic, :], AF.Square))(),
                    reads=(f"q16_{qp}_{ic}",), writes=(f"qsq{ic}",))
            for ic in range(IC):
                for hh in range(2):
                    b.op("te", (lambda ic=ic, hh=hh: lambda e: e.matmul(
                        bank(6, rows=1), sel2[:, hh:hh + 1], qsq[:, ic, :],
                        start=True, stop=True))(),
                        reads=(f"qsq{ic}", "sel2"), writes=pkeys(6))
                    slot = 2 * ic + hh
                    b.op("ve", (lambda slot=slot: _lp(lambda e: e.tensor_copy(
                        stage[0:1, slot * QTW:(slot + 1) * QTW], bank(6, rows=1))))(),
                        reads=pkeys(6), writes=("stage",))
            if qt > 0:
                q_coef_bcast(qt - 1)
            q_coef_math(qt)
        q_coef_bcast(NQT - 1)

        if sub < 1:
            b.barrier()
            pctx.close()
            actx.close()
            return
        pctx.close()
        if sub < 2:
            b.barrier()
            actx.close()
            return
        kT = asb("kT", [P, NKJ * P], BF16)
        vaug = asb("vaug", [P, NKJ * 65], BF16)
        for h in range(2):
            sm = "ld" if h == 0 else "ldg"
            war = tuple(f"xnTd{qt}_{dc}" for qt in range(NQT)
                        for dc in range(4 * h, 4 * h + 4))
            war += tuple(f"xnT{qt}" for qt in range(NQT))
            b.dma("sp", sm, (lambda h=h: lambda e: e.dma_start(
                out=kT[0:64, h * 64 * P:(h + 1) * 64 * P].rearrange(
                    "dh (r t) -> dh r t", t=1024),
                in_=kt_ag2[h].rearrange("(r dh) t -> dh r t", dh=64)))(),
                reads=(f"kt_ag{h}",), writes=(f"kT{h}",) + war)
            b.dma("sp", sm, (lambda h=h: lambda e: e.dma_start(
                out=kT[64:128, h * 64 * P:(h + 1) * 64 * P].rearrange(
                    "dh (r t) -> dh r t", t=1024),
                in_=kt_ag2[h].rearrange("(r dh) t -> dh r t", dh=64)))(),
                reads=(f"kt_ag{h}",), writes=(f"kT{h}",) + war)
        b.barrier()
        for h in range(2):
            sm = "ld" if h == 0 else "ldg"
            b.dma("sp", sm, (lambda h=h: lambda e: e.dma_start(
                out=vaug[:, h * 64 * 65:(h + 1) * 64 * 65].rearrange(
                    "p (r c) -> p r c", c=520),
                in_=v_ag2[h].rearrange("(r p) c -> p r c", p=P)))(),
                reads=(f"v_ag{h}",), writes=(f"vaug{h}",))

        attT = asb("attT", [P, IC, QTW], BF16)
        numer = asb("numer", [P, IC, QTW], F32)
        ybf = asb("ybf", [P, DC, QTW], BF16)
        ybf_v = ybf
        yqs = asb("yqs", [P, 2, QTW], BF16)
        dinv = asb("dinv", [8, QTW], F32R)
        dinw = asb("dinw", [8, QTW], F32)
        exp_sb = asb("exp_sb", [P, 3, 1024], BF16)

        def make_tail_units(qt):
            tsl = slice(qt * QTW, (qt + 1) * QTW)
            units = []

            mm_g = (qt == NQT - 1)

            def u_gather():
                emit_gather(8, dinw[0:8, :], "dinv", col0=0, mm=mm_g, pb=6)
                b.op("ve", _lp(lambda e: e.reciprocal(dinv[0:8, :], dinw[0:8, :])),
                     reads=("dinv",), writes=("dinv",))
            units.append(u_gather)

            def u_div(ic):
                b.op("te", (lambda ic=ic: lambda e: e.matmul(
                    bank(6), e2h[:, ic * P:(ic + 1) * P], dinv[0:8, :],
                    start=True, stop=True))(),
                    reads=("dinv", "e2h"), writes=pkeys(6))
                b.op("ve", (lambda ic=ic: lambda e: e.tensor_mul(
                    attT[:, ic, :], numer[:, ic, :], bank(6)))(),
                    reads=("numer",) + pkeys(6), writes=("attT",))
            for ic in range(IC):
                units.append((lambda ic=ic: lambda: u_div(ic))())

            def u_out(dc):
                py = 6 + (dc % 2)
                b.op("te", (lambda dc=dc, py=py: lambda e: e.matmul(
                    bank(py), wo_b[:, 0, dc * P:(dc + 1) * P], attT[:, 0, :],
                    start=True, stop=False))(),
                    reads=("wo_b", "attT"), writes=pkeys(py))
                for ic in range(1, IC):
                    b.op("te", (lambda dc=dc, ic=ic, py=py: lambda e: e.matmul(
                        bank(py), wo_b[:, ic, dc * P:(dc + 1) * P],
                        attT[:, ic, :],
                        start=False, stop=(ic == IC - 1)))(),
                        reads=("wo_b", "attT"), writes=pkeys(py))
                b.op("ve", (lambda dc=dc, py=py: lambda e: e.tensor_copy(
                    ybf[:, dc, :], bank(py)))(),
                    reads=pkeys(py), writes=("ybf",))
            for dc in range(DC):
                units.append((lambda dc=dc: lambda: u_out(dc))())

            def u_statS():
                for dc in range(DC):
                    b.op("te", (lambda dc=dc: lambda e: e.matmul(
                        bank(6, rows=1), onesbf[:], ybf[:, dc, :],
                        start=(dc == 0), stop=(dc == DC - 1)))(),
                        reads=("ybf", "onesbf"), writes=pkeys(6))
                b.op("ve", _lp(lambda e: e.tensor_copy(stage[0:1, 0:QTW], bank(6, rows=1))),
                     reads=pkeys(6), writes=("stage",))
            units.append(u_statS)

            def u_statQ():
                for dc in range(DC):
                    bf = dc % 2
                    if mm_g:
                        b.op("act", (lambda dc=dc, bf=bf: lambda e: e.activation(
                            yqs[:, bf, :], ybf[:, dc, :], AF.Square))(),
                            reads=("ybf",), writes=(f"yqs{bf}",))
                    else:
                        b.op("ve", (lambda dc=dc, bf=bf: lambda e: e.tensor_mul(
                            yqs[:, bf, :], ybf[:, dc, :], ybf[:, dc, :]))(),
                            reads=("ybf",), writes=(f"yqs{bf}",))
                    b.op("te", (lambda dc=dc, bf=bf: lambda e: e.matmul(
                        bank(7, rows=1), onesbf[:], yqs[:, bf, :],
                        start=(dc == 0), stop=(dc == DC - 1)))(),
                        reads=(f"yqs{bf}", "onesbf"), writes=pkeys(7))
                b.op("ve", _lp(lambda e: e.tensor_copy(stage[0:1, QTW:2 * QTW], bank(7, rows=1))),
                     reads=pkeys(7), writes=("stage",))
            units.append(u_statQ)

            def u_coef():
                emit_gather(1, crS[0:1, :], "crS", col0=0, mm=mm_g, pb=6)
                emit_gather(1, crQ[0:1, :], "crQ", col0=1, mm=mm_g, pb=7)
                b.op("ve", lambda e: e.tensor_scalar_mul(crS[0:1, :], crS[0:1, :], 1.0 / D),
                     reads=("crS",), writes=("crS",))
                b.op("ve", lambda e: e.tensor_scalar_mul(crQ[0:1, :], crQ[0:1, :], 1.0 / D),
                     reads=("crQ",), writes=("crQ",))
                b.op("ve", lambda e: e.tensor_mul(crA[0:1, :], crS[0:1, :], crS[0:1, :]),
                     reads=("crS",), writes=("crA",))
                b.op("ve", lambda e: e.tensor_sub(crQ[0:1, :], crQ[0:1, :], crA[0:1, :]),
                     reads=("crA", "crQ"), writes=("crQ",))
                b.op("act", lambda e: e.activation(
                    crA[0:1, :], crQ[0:1, :], AF.Sqrt, bias=consts[0:1, 1:2]),
                    reads=("crQ", "consts"), writes=("crA",))
                b.op("ve", lambda e: e.reciprocal(crA[0:1, :], crA[0:1, :]),
                     reads=("crA",), writes=("crA",))
                b.op("ve", lambda e: e.tensor_mul(crS[0:1, :], crS[0:1, :], crA[0:1, :]),
                     reads=("crA", "crS"), writes=("crS",))
                b.op("ve", _lp(lambda e: e.tensor_copy(coefx[32:33, :], crA[0:1, :])),
                     reads=("crA",), writes=("coefx",))
                b.op("ve", _lp(lambda e: e.tensor_copy(coefx[0:1, :], crS[0:1, :])),
                     reads=("crS",), writes=("coefx",))
                b.op("te", lambda e: e.matmul(bank(6), e36[:, 4 * P:5 * P], coefx[0:36, :],
                                              start=True, stop=True),
                     reads=("coefx", "e36"), writes=pkeys(6))
                b.op("te", lambda e: e.matmul(bank(7), e36[:, 0:P], coefx[0:36, :],
                                              start=True, stop=True),
                     reads=("coefx", "e36"), writes=pkeys(7))
                b.op("act", lambda e: e.activation(bsb[:, 0, :], bank(6), AF.Copy),
                     reads=pkeys(6), writes=("bsb0",))
                b.op("act", lambda e: e.activation(bsb[:, 1, :], bank(7), AF.Copy),
                     reads=pkeys(7), writes=("bsb1",))
            units.append(u_coef)

            def u_apply(dcs):
                for dc in dcs:
                    b.op("ve", (lambda dc=dc: lambda e: e.tensor_mul(
                        t1b[:], ybf[:, dc, :], bsb[:, 0, :]))(),
                        reads=("ybf", "bsb0"), writes=("t1",))
                    b.op("ve", (lambda dc=dc: lambda e: e.tensor_sub(
                        t1b[:], t1b[:], bsb[:, 1, :]))(),
                        reads=("t1", "bsb1"), writes=("t1",))
                    b.op("ve", (lambda dc=dc: lambda e: e.scalar_tensor_tensor(
                        xT[:, dc, tsl], t1b[:], g_out[:, dc:dc + 1], xT[:, dc, tsl],
                        ALU.mult, ALU.add))(),
                        reads=("t1", "g_out", f"xT{qt}"), writes=(f"xT{qt}",))
            units.append(lambda: u_apply(range(0, 4)))
            units.append(lambda: u_apply(range(4, 8)))
            if emit_post_stats:
                def u_stats():
                    tsl2 = slice(qt * QTW, (qt + 1) * QTW)
                    for dc in range(DC):
                        b.op("ve", (lambda dc=dc, tsl2=tsl2: lambda e: e.tensor_copy(
                            ybf[:, dc, :], xT[:, dc, tsl2]))(),
                            reads=(f"xT{qt}",), writes=("ybf",))
                    for dc in range(DC):
                        b.op("te", (lambda dc=dc: lambda e: e.matmul(
                            bank(6, rows=1), onesbf[:], ybf[:, dc, :],
                            start=(dc == 0), stop=(dc == DC - 1)))(),
                            reads=("ybf", "onesbf"), writes=pkeys(6))
                    b.op("ve", lambda e: e.tensor_copy(crB[0:1, :], bank(6, rows=1)),
                         reads=pkeys(6), writes=("crB",))
                    b.dma("sp", "ldg", (lambda qt=qt: lambda e: e.dma_start(
                        out=stg_d[(8 + qt) * QTW:(9 + qt) * QTW], in_=crB[0:1, :]))(),
                        reads=("crB",), writes=("stg_dS",))
                    for dc in range(DC):
                        b.op("ve", (lambda dc=dc: lambda e: e.tensor_mul(
                            ybf[:, dc, :], ybf[:, dc, :], ybf[:, dc, :]))(),
                            reads=("ybf",), writes=("ybf",))
                    for dc in range(DC):
                        b.op("te", (lambda dc=dc: lambda e: e.matmul(
                            bank(7, rows=1), onesbf[:], ybf[:, dc, :],
                            start=(dc == 0), stop=(dc == DC - 1)))(),
                            reads=("ybf", "onesbf"), writes=pkeys(7))
                    b.op("ve", lambda e: e.tensor_copy(crB[0:1, :], bank(7, rows=1)),
                         reads=pkeys(7), writes=("crB",))
                    b.dma("sp", "ldg", (lambda qt=qt: lambda e: e.dma_start(
                        out=stg_d[(12 + qt) * QTW:(13 + qt) * QTW], in_=crB[0:1, :]))(),
                        reads=("crB",), writes=("stg_dQ",))
                units.append(u_stats)
            return units

        pending = []
        for qt in range(NQT):
            tsl = slice(qt * QTW, (qt + 1) * QTW)
            batch = 0 if qt < 2 else 1
            chunks = list(range(batch * 64, batch * 64 + 64))
            for ic in range(IC):
                DEPTH = 2
                n = len(chunks)
                for j in range(n + DEPTH):
                    if j % 4 == 2 and pending:
                        pending.pop(0)()
                    if j < n:
                        c = chunks[j]
                        par3 = j % 3
                        sk = 2 * (j % 2)
                        b.op("te", (lambda c=c, ic=ic, tsl=tsl, sk=sk, batch=batch: lambda e: e.matmul(
                            bank(sk), kT[0:64, c * P:(c + 1) * P],
                            qT[0:64, ic, tsl], start=True, stop=True,
                            tile_position=(0, 0)))(),
                            reads=(f"kT{batch}", "qT"), writes=pkeys(sk))
                        b.op("te", (lambda c=c, ic=ic, tsl=tsl, sk=sk, batch=batch: lambda e: e.matmul(
                            bank(sk + 1), kT[64:128, c * P:(c + 1) * P],
                            qT[64:128, ic, tsl], start=True, stop=True,
                            tile_position=(64, 0)))(),
                            reads=(f"kT{batch}", "qT"), writes=pkeys(sk + 1))
                        b.op("act", (lambda par3=par3, sk=sk: lambda e: e.activation(
                            exp_sb[:, par3, :], bank(sk, nbanks=2), AF.Exp,
                            bias=consts[:, 0:1]))(),
                            reads=pkeys(sk, 2) + ("consts",), writes=(f"exp{par3}",))
                    if j >= DEPTH:
                        jj = j - DEPTH
                        cc = chunks[jj]
                        par2 = jj % 3
                        b.op("te", (lambda cc=cc, par2=par2, jj=jj, n=n, batch=batch: lambda e: e.matmul(
                            bank(4, rows=65), vaug[:, cc * 65:(cc + 1) * 65],
                            exp_sb[:, par2, 0:512],
                            start=(jj == 0), stop=(jj == n - 1)))(),
                            reads=(f"vaug{batch}", f"exp{par2}"), writes=pkeys(4))
                        b.op("te", (lambda cc=cc, par2=par2, jj=jj, n=n, batch=batch: lambda e: e.matmul(
                            bank(5, rows=65), vaug[:, cc * 65:(cc + 1) * 65],
                            exp_sb[:, par2, 512:1024],
                            start=(jj == 0), stop=(jj == n - 1)))(),
                            reads=(f"vaug{batch}", f"exp{par2}"), writes=pkeys(5))
                b.op("ve", (lambda ic=ic: lambda e: e.tensor_copy(
                    numer[0:64, ic, :], bank(4, rows=65)[0:64, :]))(),
                    reads=pkeys(4), writes=("numer",))
                b.op("ve", (lambda ic=ic: lambda e: e.tensor_copy(
                    numer[64:128, ic, :], bank(5, rows=65)[0:64, :]))(),
                    reads=pkeys(5), writes=("numer",))
                b.op("ve", (lambda ic=ic: _lp(lambda e: e.tensor_copy(
                    stage[0:1, (2 * ic) * QTW:(2 * ic + 1) * QTW],
                    bank(4, rows=65)[64:65, :])))(),
                    reads=pkeys(4), writes=("stage",))
                b.op("ve", (lambda ic=ic: _lp(lambda e: e.tensor_copy(
                    stage[0:1, (2 * ic + 1) * QTW:(2 * ic + 2) * QTW],
                    bank(5, rows=65)[64:65, :])))(),
                    reads=pkeys(5), writes=("stage",))
            for u in pending:
                u()
            pending = make_tail_units(qt)
        for u in pending:
            u()
        actx.close()
        b.barrier()

    _phases = int(os.environ.get("MK_PHASES", "3"))
    _hook_stats = os.environ.get("MK_HOOK", "1") == "1" and _phases >= 2
    if _phases >= 1:
        hk = None
        if _hook_stats:
            def hk(qt):
                emit_stats_qt(qt, _ffn_xb[0], pb=(6, 7))
                if qt == NQT - 1:
                    emit_coef_block(attn_double=True)
        emit_ffn(0, post_hook=hk)
    if _phases >= 2:
        emit_attn(sub=int(os.environ.get("MK_ATTN_SUB", "3")),
                  stats_done=_hook_stats, emit_post_stats=(_phases >= 3 and os.environ.get("MK_PSTATS", "0") == "1"))
    if _phases >= 3:
        def store_qt(qt):
            tsl = slice(qt * QTW, (qt + 1) * QTW)
            b.dma("po", "pod", (lambda qt=qt, tsl=tsl: lambda e: e.dma_start(
                out=out_d.rearrange("p (dc t) -> p dc t", t=T)[:, :, tsl],
                in_=xT[:, :, tsl]))(),
                reads=(f"xT{qt}",), writes=("out",))
        emit_ffn(1, post_hook=store_qt,
                 skip_stats=(os.environ.get("MK_PSTATS", "0") == "1"))

    if _phases < 3:
        for i in range(4):
            b.dma("po", "pod", (lambda i=i: lambda e: e.dma_start(
                out=out_d[:, i * 4096:(i + 1) * 4096],
                in_=xT.ap().rearrange("p dc t -> p (dc t)")[:, i * 4096:(i + 1) * 4096]))(),
                reads=("xT0", "xT1", "xT2", "xT3"), writes=("out",))
    b.barrier()

    b.build(nc, ctx)
    ctx.close()
    return nc


_NC_CACHE = None


def kernel(L0_attn_x, L0_attn_norm_g, L0_attn_Wq, L0_attn_Wkv, L0_attn_Wout,
           L0_attn_out_norm_g, L0_ffn_w_gate, L0_ffn_w_up, L0_ffn_w_down):
    global _NC_CACHE
    x = np.asarray(L0_attn_x, dtype=np.float32)
    norm_g = np.asarray(L0_attn_norm_g, dtype=np.float32)
    out_g = np.asarray(L0_attn_out_norm_g, dtype=np.float32)
    Wq = np.asarray(L0_attn_Wq, dtype=np.float32)
    Wkv = np.asarray(L0_attn_Wkv, dtype=np.float32)
    Wout = np.asarray(L0_attn_Wout, dtype=np.float32)
    wg = np.asarray(L0_ffn_w_gate, dtype=np.int32)
    wu = np.asarray(L0_ffn_w_up, dtype=np.int32)
    wd = np.asarray(L0_ffn_w_down, dtype=np.int32)

    wg_t = np.ascontiguousarray(wg.T.reshape(D, NHC, P).transpose(1, 0, 2))
    wu_t = np.ascontiguousarray(wu.T.reshape(D, NHC, P).transpose(1, 0, 2))
    wd_t = np.ascontiguousarray(wd.T.reshape(NHC, P, D))
    wq_t = np.ascontiguousarray(Wq.reshape(DC, P, 512))
    wkv_t = np.ascontiguousarray(Wkv.reshape(DC, P, P))
    wo_t = np.ascontiguousarray(Wout.reshape(IC, P, D))
    gatt = np.ascontiguousarray(norm_g.reshape(DC, P).T)
    gout = np.ascontiguousarray(out_g.reshape(DC, P).T)

    if _NC_CACHE is None:
        _NC_CACHE = build_program()
    nc = _NC_CACHE

    in_maps = []
    for c in range(8):
        sl = slice(c * 1024, (c + 1) * 1024)
        xc = np.concatenate([x[0, sl], x[1, sl]], axis=0)
        xc_fm = np.ascontiguousarray(xc.T)
        in_maps.append({
            "x": xc_fm, "wg": wg_t, "wu": wu_t, "wd": wd_t,
            "wq": wq_t, "wkv": wkv_t, "wo": wo_t,
            "gatt": gatt, "gout": gout,
        })
    res = run_bass_kernel_spmd(nc, in_maps, core_ids=list(range(8)))
    out = np.empty((2, 8192, D), dtype=np.float32)
    for c in range(8):
        o = np.asarray(res.results[c]["out"], dtype=np.float32)
        o = o.reshape(P, DC, T).transpose(2, 1, 0).reshape(T, D)
        sl = slice(c * 1024, (c + 1) * 1024)
        out[0, sl] = o[0:1024]
        out[1, sl] = o[1024:2048]
    return out


# version 74
# speedup vs baseline: 1.0007x; 1.0007x over previous
import os
import numpy as np
import ml_dtypes
import concourse.bass as bass
import concourse.mybir as mybir
from concourse.bass_utils import run_bass_kernel_spmd
from contextlib import ExitStack

F32 = mybir.dt.float32
BF16 = mybir.dt.bfloat16
I32 = mybir.dt.int32
F32R = mybir.dt.float32r
AF = mybir.ActivationFunctionType
ALU = mybir.AluOpType

P = 128
D = 1024
DC = 8
T = 2048
NQT = 4
QTW = 512
HID = 3072
NHC = 24
GRP = 6
NH = 8
IC = 4
DH = 64
NKJ = 128
EPS = 1e-5
DEQ = 1.0 / 127.0
RG = [[0, 1, 2, 3, 4, 5, 6, 7]]

ENGS = ("te", "act", "ve", "po", "sp")
DMA_SEMS = ("ld", "ldg", "lda", "pod", "pod0", "pod1", "pod2", "cc")
QUEUE_OF = {"ld": "sp", "ldg": "sp", "lda": "act", "pod": "po",
            "pod0": "po", "pod1": "po", "pod2": "po", "cc": "po"}


class B:

    def __init__(self):
        self.q = {e: [] for e in ENGS}
        self.cnt = {e: 0 for e in ENGS}
        self.dmac = {s: 0 for s in DMA_SEMS}
        self.knows = {}
        self.last_w = {}
        self.readers = {}
        self.fence = {s: (0, None) for s in DMA_SEMS}

    def _fence_dma(self, sem):
        covered, tok = self.fence[sem]
        total = self.dmac[sem]
        if tok is not None and covered >= total:
            return tok
        qeng = QUEUE_OF[sem]
        self.q[qeng].append(([(sem, total)], "NOP", "e"))
        self.knows[(qeng, sem)] = total
        self.cnt[qeng] += 1
        tok = (qeng, self.cnt[qeng])
        self.fence[sem] = (total, tok)
        return tok

    def _emit(self, eng, fn, deps, inc):
        seen = {}
        for tok in deps:
            if tok is None:
                continue
            sem, val = tok
            if sem in DMA_SEMS:
                if QUEUE_OF[sem] == eng:
                    val = self.dmac[sem]
                else:
                    sem, val = self._fence_dma(sem)
                    if sem == eng:
                        continue
            if sem == eng == "te":
                continue
            if self.knows.get((eng, sem), 0) >= val:
                continue
            if seen.get(sem, 0) < val:
                seen[sem] = val
        waits = []
        for s, v in seen.items():
            waits.append((s, v))
            self.knows[(eng, s)] = v
        self.q[eng].append((waits, fn, inc))
        if inc == "e":
            self.cnt[eng] += 1
            return (eng, self.cnt[eng])
        elif inc == "cc":
            self.dmac["cc"] += 1
            return ("cc", self.dmac["cc"])
        elif inc in DMA_SEMS:
            self.dmac[inc] += 16
            return (inc, self.dmac[inc])
        return None

    def _deps(self, reads, writes):
        deps = []
        for key in reads:
            if key in self.last_w:
                deps.append(self.last_w[key])
        for key in writes:
            if key in self.last_w:
                deps.append(self.last_w[key])
            deps.extend(self.readers.get(key, ()))
        return deps

    def _record(self, tok, reads, writes):
        for key in reads:
            self.readers.setdefault(key, []).append(tok)
        for key in writes:
            self.last_w[key] = tok
            self.readers[key] = []
        return tok

    def op(self, eng, fn, reads=(), writes=()):
        tok = self._emit(eng, fn, self._deps(reads, writes), "e")
        return self._record(tok, reads, writes)

    def dma(self, qeng, sem, fn, reads=(), writes=()):
        tok = self._emit(qeng, fn, self._deps(reads, writes), sem)
        return self._record(tok, reads, writes)

    def barrier(self):
        for s in DMA_SEMS:
            if self.dmac[s] > 0:
                self._fence_dma(s)
        toks = [(e, self.cnt[e]) for e in ENGS]
        for eng in ENGS:
            self._emit(eng, None, [t for t in toks if t[0] != eng], None)

    def build(self, nc, ctx):
        sems = {}
        for name in ENGS + DMA_SEMS:
            sems[name] = ctx.enter_context(nc.semaphore("s_" + name))
        block = ctx.enter_context(nc.Block())

        def mk(ename):
            def run(engine):
                for waits, fn, inc in self.q[ename]:
                    for s, v in waits:
                        engine.wait_ge(sems[s], v)
                    if fn is None:
                        continue
                    if fn == "NOP":
                        engine.nop().then_inc(sems[ename], 1)
                        continue
                    inst = fn(engine)
                    if inc == "e":
                        inst.then_inc(sems[ename], 1)
                    elif inc == "cc":
                        inst.then_inc(sems["cc"], 1)
                    elif inc in DMA_SEMS:
                        inst.then_inc(sems[inc], 16)
            return run

        block.tensor(mk("te"))
        block.scalar(mk("act"))
        block.vector(mk("ve"))
        block.gpsimd(mk("po"))
        block.sync(mk("sp"))


def build_program():
    nc = bass.Bass("TRN2", target_bir_lowering=False, num_devices=8)
    b = B()

    x_in = nc.dram_tensor("x", [D, T], F32, kind="ExternalInput").ap()
    wg_in = nc.dram_tensor("wg", [NHC, D, P], I32, kind="ExternalInput").ap()
    wu_in = nc.dram_tensor("wu", [NHC, D, P], I32, kind="ExternalInput").ap()
    wd_in = nc.dram_tensor("wd", [NHC, P, D], I32, kind="ExternalInput").ap()
    wq_in = nc.dram_tensor("wq", [DC, P, 512], F32, kind="ExternalInput").ap()
    wkv_in = nc.dram_tensor("wkv", [DC, P, P], F32, kind="ExternalInput").ap()
    wo_in = nc.dram_tensor("wo", [IC, P, D], F32, kind="ExternalInput").ap()
    gatt_in = nc.dram_tensor("gatt", [P, DC], F32, kind="ExternalInput").ap()
    gout_in = nc.dram_tensor("gout", [P, DC], F32, kind="ExternalInput").ap()
    out_d = nc.dram_tensor("out", [P, DC * T], F32, kind="ExternalOutput").ap()

    cc_kin2 = [nc.dram_tensor(f"cc_kin{h}", [64, 1024], BF16, kind="Internal").ap()
               for h in range(2)]
    cc_vin2 = [nc.dram_tensor(f"cc_vin{h}", [P, 520], BF16, kind="Internal").ap()
               for h in range(2)]
    kt_ag2 = [nc.dram_tensor(f"kt_ag{h}", [8 * 64, 1024], BF16, kind="Internal").ap()
              for h in range(2)]
    v_ag2 = [nc.dram_tensor(f"v_ag{h}", [8 * P, 520], BF16, kind="Internal").ap()
             for h in range(2)]
    stg_d = nc.dram_tensor("stg_d", [16 * QTW], F32, kind="Internal").ap()

    ident_d = nc.inline_tensor(np.eye(64, dtype=ml_dtypes.bfloat16), "identc").ap()
    identf_d = nc.inline_tensor(np.eye(64, dtype=np.float32), "identfc").ap()
    onesbf_d = nc.inline_tensor(np.ones((P, 1), dtype=ml_dtypes.bfloat16), "onesbfc").ap()
    consts_d = nc.inline_tensor(
        np.tile(np.array([[-16.0, EPS, DEQ, 16.0, 0.5, 1.0 / D]],
                         dtype=np.float32), (P, 1)),
        "constsc",
    ).ap()
    sel2_np = np.zeros((P, 2), dtype=ml_dtypes.bfloat16)
    sel2_np[0:64, 0] = 1.0
    sel2_np[64:128, 1] = 1.0
    sel2_d = nc.inline_tensor(sel2_np, "sel2c").ap()
    ones64_d = nc.inline_tensor(np.ones((64, 1), dtype=ml_dtypes.bfloat16), "ones64c").ap()
    e36_np = np.zeros((36, 8 * P), dtype=np.float32)
    for qt in range(4):
        e36_np[qt, qt * P:(qt + 1) * P] = 1.0
        e36_np[32 + qt, (4 + qt) * P:(5 + qt) * P] = 1.0
    e36_d = nc.inline_tensor(e36_np, "e36c").ap()
    e2h_np = np.zeros((8, IC * P), dtype=np.float32)
    for ic in range(IC):
        e2h_np[2 * ic, ic * P:ic * P + 64] = 1.0
        e2h_np[2 * ic + 1, ic * P + 64:(ic + 1) * P] = 1.0
    e2h_d = nc.inline_tensor(e2h_np, "e2hc").ap()
    esel_np = np.zeros((1, 96), dtype=np.float32)
    for r in range(4):
        esel_np[0, r * 4 + r] = 1.0
    for r in range(8):
        esel_np[0, 16 + r * 8 + r] = 1.0
    for r in range(2):
        esel_np[0, 80 + r * 2 + r] = 1.0
    esel_d = nc.inline_tensor(esel_np, "eselc").ap()

    ctx = ExitStack()
    sb = lambda name, shape, dt: ctx.enter_context(nc.sbuf_tensor(name, shape, dt))

    xT = sb("xT", [P, DC, T], F32)
    consts = sb("consts", [P, 6], F32)
    ident = sb("ident", [64, 64], BF16)
    identf = sb("identf", [64, 64], F32)
    onesbf = sb("onesbf", [P, 1], BF16)
    ones64 = sb("ones64", [64, 1], BF16)
    sel2 = sb("sel2", [P, 2], BF16)
    e36 = sb("e36", [36, 8 * P], F32R)
    e2h = sb("e2h", [8, IC * P], F32R)
    g_att = sb("g_att", [P, DC], F32)
    g_out = sb("g_out", [P, DC], F32)
    stage = sb("stage", [1, 8 * QTW], F32R)
    coefx = sb("coefx", [36, QTW], F32R)
    crS = sb("crS", [4, QTW], F32)
    crQ = sb("crQ", [4, QTW], F32)
    crA = sb("crA", [4, QTW], F32)
    crAr = sb("crAr", [4, QTW], F32R)
    crB = (sb("crB", [1, QTW], F32)
           if os.environ.get("MK_PSTATS", "0") == "1" else None)
    t1b = sb("t1b", [P, QTW], BF16)
    bsb = sb("bsb", [P, 2, QTW], BF16)
    wkv_b = sb("wkv_b", [P, DC, P], BF16)
    esel = sb("esel", [1, 96], F32R)
    gatt_bf = sb("gatt_bf", [P, DC], BF16)

    PSUM = ctx.enter_context(nc.psum_tensor("PSUM", [P, 4096], F32))

    def bank(k, nbanks=1, rows=P):
        return PSUM[0:rows, k * QTW:(k + nbanks) * QTW]

    def bank_bf16(k, rows=P, cols=1024):
        return PSUM.ap().bitcast(BF16)[0:rows, k * 1024:k * 1024 + cols]

    pkeys = lambda k, n=1: tuple(f"ps{j}" for j in range(k, k + n))

    def _lp(fn):
        def g(e):
            with nc.allow_low_precision("f32r rounding of broadcast operands"):
                return fn(e)
        return g

    _oneshot_gather = os.environ.get("MK_G1", "0") == "1"

    def emit_gather(nslots, dst, dst_key, col0=0, mm=False, pb=7):
        if mm:
            off = {4: 0, 8: 16, 1: 16, 2: 80}[nslots]
            for r in range(nslots):
                b.op("te", (lambda r=r, n=nslots, c0=col0, off=off, pb=pb: lambda e: e.matmul(
                    bank(pb, rows=n), esel[0:1, off + r * n:off + (r + 1) * n],
                    stage[0:1, (c0 + r) * QTW:(c0 + r + 1) * QTW],
                    start=(r == 0), stop=(r == n - 1)))(),
                    reads=("stage", "esel"), writes=pkeys(pb))
            b.op("ve", (lambda n=nslots, dst=dst, pb=pb: lambda e: e.tensor_copy(
                dst, bank(pb, rows=n)))(),
                reads=pkeys(pb), writes=(dst_key,))
            return
        if _oneshot_gather:
            b.dma("sp", "ldg", (lambda n=nslots, c0=col0, dst=dst: lambda e: e.dma_start(
                out=dst,
                in_=stage.ap().bitcast(F32)[0:1, c0 * QTW:(c0 + n) * QTW].rearrange(
                    "o (r n) -> (o r) n", n=QTW)))(),
                reads=("stage",), writes=(dst_key,))
            return
        b.dma("sp", "ldg", (lambda n=nslots, c0=col0: lambda e: e.dma_start(
            out=stg_d[0:n * QTW],
            in_=stage.ap().bitcast(F32)[0:1, c0 * QTW:(c0 + n) * QTW]))(),
            reads=("stage",), writes=("stg_d",))
        b.dma("sp", "ldg", (lambda n=nslots, dst=dst: lambda e: e.dma_start(
            out=dst, in_=stg_d[0:n * QTW].rearrange("(r n) -> r n", n=QTW)))(),
            reads=("stg_d",), writes=(dst_key,))

    _xq = [("sp", "ld"), ("sp", "ldg"), ("po", "pod2"), ("sp", "ld")]
    for qt in range(NQT):
        tsl = slice(qt * QTW, (qt + 1) * QTW)
        qe, sm = _xq[qt]
        b.dma(qe, sm, (lambda tsl=tsl, qt=qt: lambda e: e.dma_start(
            out=xT[:, :, tsl],
            in_=x_in.rearrange("(dc p) t -> p dc t", p=P)[:, :, tsl]))(),
            writes=(f"xT{qt}",))
    _pending_selectors = []
    for dst, src, key in (
        (consts, consts_d, "consts"), (ident, ident_d, "ident"),
        (identf, identf_d, "identf"),
        (onesbf, onesbf_d, "onesbf"), (ones64, ones64_d, "ones64"),
        (sel2, sel2_d, "sel2"), (e36, e36_d, "e36"),
        (e2h, e2h_d, "e2h"), (esel, esel_d, "esel"),
        (g_att, gatt_in, "g_att"), (g_out, gout_in, "g_out"),
    ):
        if dst.dtype == F32R:
            _pending_selectors.append((dst, src, key))
        else:
            b.dma("sp", "ld", (lambda d, s: lambda e: e.dma_start(
                out=d[:], in_=s))(dst, src),
                writes=(key,))

    def load_wkv():
        b.dma("po", "pod1", lambda e: e.dma_start(
            out=wkv_b[:], in_=wkv_in.rearrange("dc p w -> p dc w")),
            writes=("wkv_b",))
    b.op("ve", lambda e: e.tensor_copy(gatt_bf[:], g_att[:]),
         reads=("g_att",), writes=("gatt_bf",))

    _ffn_xb = [None]



    def emit_stats_qt(qt, xb_sc, pb=(6, 7)):
        tsl = slice(qt * QTW, (qt + 1) * QTW)
        for dc in range(DC):
            bf = dc % 2
            b.op("ve", (lambda bf=bf, dc=dc, tsl=tsl, qt=qt: lambda e: e.tensor_copy(
                xb_sc[:, 0, bf, :], xT[:, dc, tsl]))(),
                reads=(f"xT{qt}",), writes=(f"xb{bf}",))
            b.op("act", (lambda bf=bf, xb_sc=xb_sc: lambda e: e.activation(
                xb_sc[:, 1, bf, :], xb_sc[:, 0, bf, :], AF.Square))(),
                reads=(f"xb{bf}",), writes=(f"xq{bf}",))
            b.op("te", (lambda bf=bf, dc=dc, xb_sc=xb_sc, pb=pb: lambda e: e.matmul(
                bank(pb[0], rows=1), onesbf[:], xb_sc[:, 0, bf, :],
                start=(dc == 0), stop=(dc == DC - 1)))(),
                reads=(f"xb{bf}", "onesbf"), writes=pkeys(pb[0]))
            b.op("te", (lambda bf=bf, dc=dc, xb_sc=xb_sc, pb=pb: lambda e: e.matmul(
                bank(pb[1], rows=1), onesbf[:], xb_sc[:, 1, bf, :],
                start=(dc == 0), stop=(dc == DC - 1)))(),
                reads=(f"xq{bf}", "onesbf"), writes=pkeys(pb[1]))
        b.op("ve", (lambda qt=qt, pb=pb: _lp(lambda e: e.tensor_copy(
            stage[0:1, qt * QTW:(qt + 1) * QTW], bank(pb[0], rows=1))))(),
            reads=pkeys(pb[0]), writes=("stage",))
        b.op("ve", (lambda qt=qt, pb=pb: _lp(lambda e: e.tensor_copy(
            stage[0:1, (4 + qt) * QTW:(5 + qt) * QTW], bank(pb[1], rows=1))))(),
            reads=pkeys(pb[1]), writes=("stage",))

    def emit_coef_block(attn_double):
        emit_gather(4, crS[0:4, :], "crS", col0=0, mm=True, pb=6)
        emit_gather(4, crQ[0:4, :], "crQ", col0=4, mm=True, pb=7)
        _emit_coef_math(attn_double)

    def emit_ln_coefs(tag, attn_double, xb_sc=None, stats_mode="compute"):
        if stats_mode == "done":
            return
        if stats_mode == "dram":
            b.dma("sp", "ldg", lambda e: e.dma_start(
                out=crS[0:4, :],
                in_=stg_d[8 * QTW:12 * QTW].rearrange("(r n) -> r n", n=QTW)),
                reads=("stg_dS",), writes=("crS",))
            b.dma("sp", "ldg", lambda e: e.dma_start(
                out=crQ[0:4, :],
                in_=stg_d[12 * QTW:16 * QTW].rearrange("(r n) -> r n", n=QTW)),
                reads=("stg_dQ",), writes=("crQ",))
        else:
            if stats_mode == "compute":
                for qt in range(NQT):
                    emit_stats_qt(qt, xb_sc)
            emit_gather(4, crS[0:4, :], "crS", col0=0, mm=True, pb=6)
            emit_gather(4, crQ[0:4, :], "crQ", col0=4, mm=True, pb=7)
        _emit_coef_math(attn_double)

    def _emit_coef_math(attn_double):
        b.op("ve", lambda e: e.tensor_scalar_mul(crS[:], crS[:], 1.0 / D),
             reads=("crS",), writes=("crS",))
        b.op("ve", lambda e: e.tensor_scalar_mul(crQ[:], crQ[:], 1.0 / D),
             reads=("crQ",), writes=("crQ",))
        b.op("ve", lambda e: e.tensor_mul(crA[:], crS[:], crS[:]),
             reads=("crS",), writes=("crA",))
        b.op("ve", lambda e: e.tensor_sub(crQ[:], crQ[:], crA[:]),
             reads=("crA", "crQ"), writes=("crQ",))
        if attn_double:
            b.op("ve", lambda e: e.tensor_scalar(
                crA[:], crQ[:], 1.0 + EPS, EPS * EPS, ALU.mult, ALU.add),
                reads=("crQ",), writes=("crA",))
            b.op("act", lambda e: e.activation(crA[:], crA[:], AF.Sqrt),
                 reads=("crA",), writes=("crA",))
        else:
            b.op("act", lambda e: e.activation(
                crA[:], crQ[:], AF.Sqrt, bias=consts[0:4, 1:2]),
                reads=("crQ", "consts"), writes=("crA",))
        b.op("ve", lambda e: e.reciprocal(crA[:], crA[:]),
             reads=("crA",), writes=("crA",))
        b.op("ve", lambda e: e.tensor_mul(crS[:], crS[:], crA[:]),
             reads=("crA", "crS"), writes=("crS",))
        b.op("ve", _lp(lambda e: e.tensor_copy(coefx[32:36, :], crA[:])),
             reads=("crA",), writes=("coefx",))
        b.op("ve", _lp(lambda e: e.tensor_copy(coefx[0:4, :], crS[:])),
             reads=("crS",), writes=("coefx",))

    def emit_ln_apply(xnT, gain, after_qt=None):
        assert gain is None
        for qt in range(NQT):
            tsl = slice(qt * QTW, (qt + 1) * QTW)
            pa, pb_ = (6, 7) if qt % 2 == 0 else (4, 5)
            par = qt % 2
            b.op("te", (lambda qt=qt, pa=pa: lambda e: e.matmul(
                bank(pa), e36[:, (4 + qt) * P:(5 + qt) * P], coefx[0:36, :],
                start=True, stop=True))(),
                reads=("coefx", "e36"), writes=pkeys(pa))
            b.op("te", (lambda qt=qt, pb_=pb_: lambda e: e.matmul(
                bank(pb_), e36[:, qt * P:(qt + 1) * P], coefx[0:36, :],
                start=True, stop=True))(),
                reads=("coefx", "e36"), writes=pkeys(pb_))
            b.op("act", (lambda pb_=pb_, par=par: lambda e: e.activation(
                bsb[:, par, :], bank(pb_), AF.Copy))(),
                reads=pkeys(pb_), writes=(f"bsb{par}",))
            if after_qt is not None and qt > 0:
                after_qt(qt - 1)
            for dc in range(DC):
                b.op("ve", (lambda dc=dc, tsl=tsl, pa=pa: lambda e: e.tensor_mul(
                    t1b[:], xT[:, dc, tsl], bank(pa)))(),
                    reads=(f"xT{qt}",) + pkeys(pa), writes=("t1",))
                b.op("ve", (lambda dc=dc, tsl=tsl, par=par: lambda e: e.tensor_sub(
                    xnT[:, dc, tsl], t1b[:], bsb[:, par, :]))(),
                    reads=("t1", f"bsb{par}"), writes=(f"xnT{qt}",))
        if after_qt is not None:
            after_qt(NQT - 1)

    def emit_ffn(pass_id, post_hook=None, skip_stats=False):
        with ExitStack() as fctx:
            fsb = lambda name, shape, dt: fctx.enter_context(
                nc.sbuf_tensor(f"{name}_{pass_id}", shape, dt))
            xnT = fsb("xnT", [P, DC, T], BF16)
            xb_f = fsb("xb_f", [P, 2, 2, QTW], BF16)
            _ffn_xb[0] = xb_f
            if pass_id == 0:
                selstg_t = fsb("selstg", [36, 8 * P], F32)
                selstg = selstg_t.ap()
                for dst, srcap, key in _pending_selectors:
                    nr, ncol = dst.shape
                    b.dma("sp", "ld", (lambda s=srcap, nr=nr, ncol=ncol: lambda e: e.dma_start(
                        out=selstg[0:nr, 0:ncol], in_=s))(),
                        reads=("xb",), writes=("xb",))
                    b.op("ve", (lambda d=dst, nr=nr, ncol=ncol: _lp(lambda e: e.tensor_copy(
                        d[:], selstg[0:nr, 0:ncol])))(),
                        reads=("xb",), writes=(key, "xb"))
                b.op("ve", lambda e: e.memset(selstg[0:36, 0:QTW], 0.0),
                     reads=("xb",), writes=("xb",))
                b.op("ve", _lp(lambda e: e.tensor_copy(coefx[:], selstg[0:36, 0:QTW])),
                     reads=("xb",), writes=("coefx", "xb"))
            wgb = fsb("wgb", [P, 3, D], BF16)
            wub = fsb("wub", [P, 3, D], BF16)
            wdb = fsb("wdb", [P, GRP + 2, D], BF16)
            hT = fsb("hT", [P, GRP, T], BF16)
            sg = fsb("sg", [P, 2, QTW], BF16)
            su = fsb("su", [P, 2, QTW], BF16)

            def load_wg(hc, buf):
                b.dma("po", f"pod{hc % 3}", (lambda hc=hc, buf=buf: lambda e: e.dma_start(
                    out=wgb[:, buf, :].rearrange("p (dc h) -> p dc h", h=P),
                    in_=wg_in[hc].rearrange("(dc p) h -> p dc h", p=P)))(),
                    writes=(f"wgb{buf}",))

            def load_wu(hc, buf):
                b.dma("po", f"pod{hc % 3}", (lambda hc=hc, buf=buf: lambda e: e.dma_start(
                    out=wub[:, buf, :].rearrange("p (dc h) -> p dc h", h=P),
                    in_=wu_in[hc].rearrange("(dc p) h -> p dc h", p=P)))(),
                    writes=(f"wub{buf}",))

            def load_gu(hc, buf):
                load_wg(hc, buf)
                load_wu(hc, buf)

            def load_wd(hc):
                b.dma("po", f"pod{hc % 3}", (lambda hc=hc: lambda e: e.dma_start(
                    out=wdb[:, hc % (GRP + 2), :], in_=wd_in[hc]))(),
                    writes=(f"wdb{hc % (GRP + 2)}",))

            load_gu(0, 0)
            load_wd(0)

            def gu_mms(j, buf, qt):
                tsl = slice(qt * QTW, (qt + 1) * QTW)
                pg, pu = (0, 1) if (qt % 2 == 0) else (2, 3)
                for dc in range(DC):
                    b.op("te", (lambda dc=dc, buf=buf, tsl=tsl, pg=pg, qt=qt: lambda e: e.matmul(
                        bank(pg), wgb[:, buf, dc * P:(dc + 1) * P],
                        xnT[:, dc, tsl],
                        start=(dc == 0), stop=(dc == DC - 1)))(),
                        reads=(f"wgb{buf}", f"xnT{qt}"), writes=pkeys(pg))
                for dc in range(DC):
                    b.op("te", (lambda dc=dc, buf=buf, tsl=tsl, pu=pu, qt=qt: lambda e: e.matmul(
                        bank(pu), wub[:, buf, dc * P:(dc + 1) * P],
                        xnT[:, dc, tsl],
                        start=(dc == 0), stop=(dc == DC - 1)))(),
                        reads=(f"wub{buf}", f"xnT{qt}"), writes=pkeys(pu))

            def gu_evac(j, qt):
                tsl = slice(qt * QTW, (qt + 1) * QTW)
                pg, pu = (0, 1) if (qt % 2 == 0) else (2, 3)
                sb_par = qt % 2
                b.op("act", (lambda pg=pg, sb_par=sb_par: lambda e: e.activation(
                    sg[:, sb_par, :], bank(pg), AF.Silu, scale=consts[:, 2:3]))(),
                    reads=pkeys(pg) + ("consts",), writes=(f"sg{sb_par}",))
                b.op("act", (lambda pu=pu, sb_par=sb_par: lambda e: e.activation(
                    su[:, sb_par, :], bank(pu), AF.Copy, scale=consts[:, 2:3]))(),
                    reads=pkeys(pu) + ("consts",), writes=(f"su{sb_par}",))
                b.op("ve", (lambda j=j, tsl=tsl, sb_par=sb_par: lambda e: e.tensor_mul(
                    hT[:, j, tsl], sg[:, sb_par, :], su[:, sb_par, :]))(),
                    reads=(f"sg{sb_par}", f"su{sb_par}"), writes=(f"hT{j}",))

            def gu_qt(j, buf, qt):
                gu_mms(j, buf, qt)
                gu_evac(j, qt)

            _h0_hoisted = os.environ.get("MK_NO_LN", "0") != "1"
            if os.environ.get("MK_NO_LN", "0") == "1":
                b.op("ve", lambda e: e.memset(
                    xnT.ap().rearrange("p a b -> p (a b)"), 0.01),
                    writes=("xnT0", "xnT1", "xnT2", "xnT3"))
            else:
                emit_ln_coefs(f"ffn{pass_id}", attn_double=False, xb_sc=xb_f,
                              stats_mode=("dram" if skip_stats else "compute"))
                emit_ln_apply(xnT, gain=None,
                              after_qt=lambda qt: (gu_mms(0, 0, qt)
                                                   if qt < 2 else None))

            for g in range(NHC // GRP):
                for j in range(GRP):
                    hc = g * GRP + j
                    buf = hc % 3
                    if pass_id == 0 and hc == GRP:
                        load_wkv()
                    for qt in range(NQT):
                        if hc == 0 and qt == 3:
                            load_gu(1, 1)
                            load_wd(1)
                        if hc + 2 < NHC:
                            if qt == 0:
                                load_wg(hc + 2, (hc + 2) % 3)
                            elif qt == 1:
                                load_wu(hc + 2, (hc + 2) % 3)
                            elif qt == 2:
                                load_wd(hc + 2)
                        if hc == 0 and _h0_hoisted and qt < 2:
                            gu_evac(j, qt)
                        else:
                            gu_qt(j, buf, qt)
                last_group = (g == NHC // GRP - 1)
                for qt in range(NQT):
                    tsl = slice(qt * QTW, (qt + 1) * QTW)
                    for dc in range(DC):
                        py = 4 + (dc % 2)
                        for j in range(GRP):
                            hcj = g * GRP + j
                            slot = hcj % (GRP + 2)
                            b.op("te", (lambda j=j, slot=slot, dc=dc, tsl=tsl, py=py: lambda e: e.matmul(
                                bank(py), wdb[:, slot, dc * P:(dc + 1) * P],
                                hT[:, j, tsl],
                                start=(j == 0), stop=(j == GRP - 1)))(),
                                reads=(f"wdb{slot}", f"hT{j}"), writes=pkeys(py))
                        b.op("ve", (lambda dc=dc, tsl=tsl, py=py: lambda e: e.scalar_tensor_tensor(
                            xT[:, dc, tsl], bank(py), 0.5 * DEQ, xT[:, dc, tsl],
                            ALU.mult, ALU.add))(),
                            reads=pkeys(py) + (f"xT{qt}",), writes=(f"xT{qt}",))
                    if last_group and dc == DC - 1 and post_hook is not None:
                        post_hook(qt)

        b.barrier()

    def emit_attn(sub=3, stats_done=False, emit_post_stats=False):
        actx = ExitStack()
        asb = lambda name, shape, dt: actx.enter_context(nc.sbuf_tensor(name, shape, dt))
        qT = asb("qT", [P, IC, T], BF16)
        wo_b = asb("wo_b", [P, IC, D], BF16)
        pctx = ExitStack()
        psb = lambda name, shape, dt: pctx.enter_context(nc.sbuf_tensor(name, shape, dt))
        xnT = psb("xnT_a", [P, DC, T], BF16)
        xb_a = None if stats_done else psb("xb_a", [P, 2, 2, QTW], BF16)
        kq = psb("kq", [9, QTW], F32R)
        kqw = psb("kqw", [8, QTW], F32)
        q16 = psb("q16", [P, 2, IC, QTW], BF16)
        scr8k = psb("scr8k", [P, T], F32)
        kvf = scr8k.ap().rearrange("p (a b) -> p a b", a=NQT)
        wstg = scr8k.ap().rearrange("p (a b) -> p a b", a=2)
        ktown = psb("ktown", [64, T], BF16)
        vtb = psb("vtb", [64, P], F32)
        vtok = psb("vtok", [P, 16, 65], BF16)
        qsq = psb("qsq", [P, IC, QTW], BF16)
        ksq = psb("ksq", [64, QTW], BF16)
        wq_b = psb("wq_b", [P, DC, 512], BF16)

        prep_lvl = int(os.environ.get("MK_PREP", "9"))

        def load_wq():
            b.dma("po", "pod0", lambda e: e.dma_start(
                out=wq_b[:], in_=wq_in.rearrange("dc p w -> p dc w")),
                writes=("wq_b",))
        b.op("ve", lambda e: e.memset(vtok.ap().rearrange("p a b -> p (a b)")[:, :], 0.0),
             writes=("vtok",))
        for t in range(16):
            b.op("ve", (lambda t=t: lambda e: e.memset(vtok[:, t, 64:65], 1.0))(),
                 reads=(), writes=("vtok",))

        emit_ln_coefs("attn", attn_double=True, xb_sc=xb_a,
                      stats_mode=("done" if stats_done else "compute"))

        m_bf = psb("m_bf", [1, T], BF16)
        aTc_sb = psb("aTc_sb", [P, 16], F32)
        cqn_bf = psb("cqn_bf", [1, IC * P + P], BF16)
        b.op("act", lambda e: e.activation(
            m_bf[0:1, :], stage[0:1, 0:NQT * QTW], AF.Copy,
            scale=consts[0:1, 5:6]),
            reads=("stage", "consts"), writes=("m_bf",))
        for dc in range(DC):
            b.op("te", (lambda dc=dc: lambda e: e.matmul(
                PSUM[0:1, 5 * QTW:5 * QTW + P],
                gatt_bf[:, dc:dc + 1], wkv_b[:, dc, :],
                start=(dc == 0), stop=(dc == DC - 1)))(),
                reads=("gatt_bf", "wkv_b"), writes=pkeys(5))
        b.op("ve", lambda e: e.tensor_scalar_mul(
            cqn_bf[0:1, IC * P:IC * P + P], PSUM[0:1, 5 * QTW:5 * QTW + P], -1.0),
            reads=pkeys(5), writes=("cqn",))

        def emit_cq():
            for ic in range(IC):
                for dc in range(DC):
                    b.op("te", (lambda ic=ic, dc=dc: lambda e: e.matmul(
                        PSUM[0:1, 4 * QTW + ic * P:4 * QTW + (ic + 1) * P],
                        gatt_bf[:, dc:dc + 1], wq_b[:, dc, ic * P:(ic + 1) * P],
                        start=(dc == 0), stop=(dc == DC - 1)))(),
                        reads=("gatt_bf", "wq_b"), writes=pkeys(4))
            b.op("ve", lambda e: e.tensor_scalar_mul(
                cqn_bf[0:1, 0:IC * P], PSUM[0:1, 4 * QTW:4 * QTW + IC * P], -1.0),
                reads=pkeys(4), writes=("cqn",))
        for tc in range(4):
            b.op("te", (lambda tc=tc: lambda e: e.transpose(
                PSUM[0:P, 5 * QTW + 256 + tc * 4:5 * QTW + 256 + (tc + 1) * 4],
                crA[0:4, tc * P:(tc + 1) * P], identf[0:4, 0:4]))(),
                reads=("crA", "identf"), writes=pkeys(5))
        b.op("ve", lambda e: e.tensor_copy(
            aTc_sb[:], PSUM[0:P, 5 * QTW + 256:5 * QTW + 272]),
            reads=pkeys(5), writes=("aTc",))

        def kv_mms(qt):
            tsl = slice(qt * QTW, (qt + 1) * QTW)
            pk = qt % 2
            for dc in range(DC):
                b.op("te", (lambda dc=dc, tsl=tsl, pk=pk, qt=qt: lambda e: e.matmul(
                    bank(pk), wkv_b[:, dc, :], xnT[:, dc, tsl],
                    start=(dc == 0), stop=False))(),
                    reads=("wkv_b", f"xnTd{qt}_{dc}"), writes=pkeys(pk))
            b.op("te", (lambda tsl=tsl, pk=pk: lambda e: e.matmul(
                bank(pk), cqn_bf[0:1, IC * P:IC * P + P], m_bf[0:1, tsl],
                start=False, stop=True))(),
                reads=("cqn", "m_bf"), writes=pkeys(pk))

        def kv_post(qt):
            tsl = slice(qt * QTW, (qt + 1) * QTW)
            pk = qt % 2
            b.op("act", (lambda qt=qt, pk=pk: lambda e: e.activation(
                kvf[:, qt, :], bank(pk), AF.Copy))(),
                reads=pkeys(pk), writes=("scr8k",))
            b.op("act", (lambda qt=qt: lambda e: e.activation(
                ksq[:], kvf[0:64, qt, :], AF.Square))(),
                reads=("scr8k",), writes=("ksq",))
            b.op("te", lambda e: e.matmul(bank(6, rows=1), ones64[:], ksq[:],
                                          start=True, stop=True),
                 reads=("ksq", "ones64"), writes=pkeys(6))
            b.op("ve", (lambda qt=qt: _lp(lambda e: e.tensor_copy(
                stage[0:1, qt * QTW:(qt + 1) * QTW], bank(6, rows=1))))(),
                reads=pkeys(6), writes=("stage",))
            for tc in range(4):
                csl = slice(tc * P, (tc + 1) * P)
                pv_ = 2 + (tc % 2)
                b.op("ve", (lambda qt=qt, csl=csl: lambda e: e.tensor_copy(
                    vtb[:], kvf[64:128, qt, csl]))(),
                    reads=("scr8k",), writes=("vtb",))
                b.op("te", (lambda pv_=pv_: lambda e: e.transpose(
                    bank(pv_)[0:P, 0:64], vtb[:], identf[:]))(),
                    reads=("vtb", "identf"), writes=pkeys(pv_))
                tile = qt * 4 + tc
                acol = (tile % 4) * 4 + tile // 4
                b.op("act", (lambda tile=tile, pv_=pv_, acol=acol: lambda e: e.activation(
                    vtok[:, tile, 0:64], bank(pv_)[0:P, 0:64], AF.Copy,
                    scale=aTc_sb[:, acol:acol + 1]))(),
                    reads=pkeys(pv_) + ("aTc",), writes=("vtok",))

        def k_half(h):
            b.dma("sp", "ld", (lambda h=h: lambda e: e.dma_start(
                out=cc_vin2[h],
                in_=vtok[:, 8 * h:8 * (h + 1), :].rearrange("p a b -> p (a b)")))(),
                reads=("vtok",), writes=(f"cc_vin{h}",))
            b.dma("po", "cc", (lambda h=h: lambda e: e.collective_compute(
                "AllGather", ALU.bypass, ins=[cc_vin2[h].opt()],
                outs=[v_ag2[h].opt()], replica_groups=RG))(),
                reads=(f"cc_vin{h}",), writes=(f"v_ag{h}",))
            emit_gather(2, crA[0:2, :], "crA", col0=2 * h, mm=True, pb=7)
            b.op("ve", lambda e: e.tensor_scalar_max(crA[0:2, :], crA[0:2, :], 1e-24),
                 reads=("crA",), writes=("crA",))
            b.op("act", lambda e: e.activation(crA[0:2, :], crA[0:2, :], AF.Sqrt),
                 reads=("crA",), writes=("crA",))
            b.op("ve", lambda e: e.tensor_scalar_mul(crA[0:2, :], crA[0:2, :], 0.25),
                 reads=("crA",), writes=("crA",))
            b.op("ve", _lp(lambda e: e.reciprocal(crAr[0:2, :], crA[0:2, :])),
                 reads=("crA",), writes=("crAr",))
            for qt in (2 * h, 2 * h + 1):
                tsl = slice(qt * QTW, (qt + 1) * QTW)
                pb = 6 + (qt % 2)
                b.op("te", (lambda qt=qt, pb=pb: lambda e: e.matmul(
                    bank(pb), e36[0:2, (qt % 2) * P:(qt % 2 + 1) * P],
                    crAr[0:2, :], start=True, stop=True))(),
                    reads=("crAr", "e36"), writes=pkeys(pb))
                b.op("ve", (lambda qt=qt, tsl=tsl, pb=pb: lambda e: e.tensor_mul(
                    ktown[:, tsl], kvf[0:64, qt, :], bank(pb)[0:64, :]))(),
                    reads=("scr8k",) + pkeys(pb), writes=("ktown",))
            b.dma("sp", "ld", (lambda h=h: lambda e: e.dma_start(
                out=cc_kin2[h], in_=ktown[:, h * 1024:(h + 1) * 1024]))(),
                reads=("ktown",), writes=(f"cc_kin{h}",))
            b.dma("po", "cc", (lambda h=h: lambda e: e.collective_compute(
                "AllGather", ALU.bypass, ins=[cc_kin2[h].opt()],
                outs=[kt_ag2[h].opt()], replica_groups=RG))(),
                reads=(f"cc_kin{h}",), writes=(f"kt_ag{h}",))


        for qt in range(NQT):
            tsl = slice(qt * QTW, (qt + 1) * QTW)
            for dc in range(DC):
                if dc % 2 == 0:
                    b.op("act", (lambda dc=dc, tsl=tsl, qt=qt: lambda e: e.activation(
                        xnT[:, dc, tsl], xT[:, dc, tsl], AF.Copy,
                        scale=g_att[:, dc:dc + 1]))(),
                        reads=(f"xT{qt}", "g_att"),
                        writes=(f"xnT{qt}", f"xnTd{qt}_{dc}"))
                else:
                    b.op("ve", (lambda dc=dc, tsl=tsl, qt=qt: lambda e: e.tensor_scalar_mul(
                        xnT[:, dc, tsl], xT[:, dc, tsl], g_att[:, dc:dc + 1]))(),
                        reads=(f"xT{qt}", "g_att"),
                        writes=(f"xnT{qt}", f"xnTd{qt}_{dc}"))
            kv_mms(qt)
            if qt > 0:
                kv_post(qt - 1)
            if qt == 2:
                k_half(0)
                load_wq()
        kv_post(NQT - 1)
        k_half(1)
        b.dma("po", "pod2", lambda e: e.dma_start(
            out=wo_b[:], in_=wo_in.rearrange("ic p w -> p ic w")),
            writes=("wo_b",))

        emit_cq()

        def q_coef_math(qt):
            emit_gather(8, kqw[0:8, :], "kq", col0=0, mm=True, pb=7)
            b.op("ve", lambda e: e.tensor_scalar_max(kqw[0:8, :], kqw[0:8, :], 1e-24),
                 reads=("kq",), writes=("kq",))
            b.op("act", lambda e: e.activation(kqw[0:8, :], kqw[0:8, :], AF.Sqrt),
                 reads=("kq",), writes=("kq",))
            b.op("ve", lambda e: e.tensor_scalar_mul(kqw[0:8, :], kqw[0:8, :], 0.25),
                 reads=("kq",), writes=("kq",))
            b.op("ve", _lp(lambda e: e.reciprocal(kq[0:8, :], kqw[0:8, :])),
                 reads=("kq",), writes=("kq",))

        def q_coef_bcast(qt):
            tsl = slice(qt * QTW, (qt + 1) * QTW)
            qp = qt % 2
            for ic in range(IC):
                pb = 6 + (ic % 2)
                b.op("te", (lambda ic=ic, pb=pb: lambda e: e.matmul(
                    bank(pb), e2h[:, ic * P:(ic + 1) * P], kq[0:8, :],
                    start=True, stop=True))(),
                    reads=("kq", "e2h"), writes=pkeys(pb))
                b.op("ve", (lambda ic=ic, tsl=tsl, pb=pb, qp=qp: lambda e: e.tensor_mul(
                    qT[:, ic, tsl], q16[:, qp, ic, :], bank(pb)))(),
                    reads=(f"q16_{qp}_{ic}",) + pkeys(pb), writes=("qT",))

        for qt in range(NQT):
            tsl = slice(qt * QTW, (qt + 1) * QTW)
            qp = qt % 2
            for ic in range(IC):
                pq = ic
                for dc in range(DC):
                    b.op("te", (lambda dc=dc, ic=ic, tsl=tsl, pq=pq, qt=qt: lambda e: e.matmul(
                        bank(pq), wq_b[:, dc, ic * P:(ic + 1) * P], xnT[:, dc, tsl],
                        start=(dc == 0), stop=False))(),
                        reads=("wq_b", f"xnTd{qt}_{dc}"), writes=pkeys(pq))
                b.op("te", (lambda ic=ic, tsl=tsl, pq=pq: lambda e: e.matmul(
                    bank(pq), cqn_bf[0:1, ic * P:(ic + 1) * P], m_bf[0:1, tsl],
                    start=False, stop=True))(),
                    reads=("cqn", "m_bf"), writes=pkeys(pq))
                b.op("act", (lambda ic=ic, pq=pq, qp=qp: lambda e: e.activation(
                    q16[:, qp, ic, :], bank(pq), AF.Copy,
                    scale=consts[:, 3:4]))(),
                    reads=pkeys(pq) + ("consts",), writes=(f"q16_{qp}_{ic}",))
                b.op("act", (lambda ic=ic, qp=qp: lambda e: e.activation(
                    qsq[:, ic, :], q16[:, qp, ic, :], AF.Square))(),
                    reads=(f"q16_{qp}_{ic}",), writes=(f"qsq{ic}",))
            for ic in range(IC):
                for hh in range(2):
                    b.op("te", (lambda ic=ic, hh=hh: lambda e: e.matmul(
                        bank(6, rows=1), sel2[:, hh:hh + 1], qsq[:, ic, :],
                        start=True, stop=True))(),
                        reads=(f"qsq{ic}", "sel2"), writes=pkeys(6))
                    slot = 2 * ic + hh
                    b.op("ve", (lambda slot=slot: _lp(lambda e: e.tensor_copy(
                        stage[0:1, slot * QTW:(slot + 1) * QTW], bank(6, rows=1))))(),
                        reads=pkeys(6), writes=("stage",))
            if qt > 0:
                q_coef_bcast(qt - 1)
            q_coef_math(qt)
        q_coef_bcast(NQT - 1)

        if sub < 1:
            b.barrier()
            pctx.close()
            actx.close()
            return
        pctx.close()
        if sub < 2:
            b.barrier()
            actx.close()
            return
        kT = asb("kT", [P, NKJ * P], BF16)
        vaug = asb("vaug", [P, NKJ * 65], BF16)
        for h in range(2):
            sm = "ld" if h == 0 else "ldg"
            war = tuple(f"xnTd{qt}_{dc}" for qt in range(NQT)
                        for dc in range(4 * h, 4 * h + 4))
            war += tuple(f"xnT{qt}" for qt in range(NQT))
            b.dma("sp", sm, (lambda h=h: lambda e: e.dma_start(
                out=kT[0:64, h * 64 * P:(h + 1) * 64 * P].rearrange(
                    "dh (r t) -> dh r t", t=1024),
                in_=kt_ag2[h].rearrange("(r dh) t -> dh r t", dh=64)))(),
                reads=(f"kt_ag{h}",), writes=(f"kT{h}",) + war)
            b.dma("sp", sm, (lambda h=h: lambda e: e.dma_start(
                out=kT[64:128, h * 64 * P:(h + 1) * 64 * P].rearrange(
                    "dh (r t) -> dh r t", t=1024),
                in_=kt_ag2[h].rearrange("(r dh) t -> dh r t", dh=64)))(),
                reads=(f"kt_ag{h}",), writes=(f"kT{h}",) + war)
        b.barrier()
        for h in range(2):
            sm = "ld" if h == 0 else "ldg"
            b.dma("sp", sm, (lambda h=h: lambda e: e.dma_start(
                out=vaug[:, h * 64 * 65:(h + 1) * 64 * 65].rearrange(
                    "p (r c) -> p r c", c=520),
                in_=v_ag2[h].rearrange("(r p) c -> p r c", p=P)))(),
                reads=(f"v_ag{h}",), writes=(f"vaug{h}",))

        attT = asb("attT", [P, IC, QTW], BF16)
        numer = asb("numer", [P, IC, QTW], F32)
        ybf = asb("ybf", [P, DC, QTW], BF16)
        ybf_v = ybf
        yqs = asb("yqs", [P, 2, QTW], BF16)
        dinv = asb("dinv", [8, QTW], F32R)
        dinw = asb("dinw", [8, QTW], F32)
        exp_sb = asb("exp_sb", [P, 3, 1024], BF16)

        def make_tail_units(qt):
            tsl = slice(qt * QTW, (qt + 1) * QTW)
            units = []

            mm_g = (qt == NQT - 1)

            def u_gather():
                emit_gather(8, dinw[0:8, :], "dinv", col0=0, mm=mm_g, pb=6)
                b.op("ve", _lp(lambda e: e.reciprocal(dinv[0:8, :], dinw[0:8, :])),
                     reads=("dinv",), writes=("dinv",))
            units.append(u_gather)

            def u_div(ic):
                b.op("te", (lambda ic=ic: lambda e: e.matmul(
                    bank(6), e2h[:, ic * P:(ic + 1) * P], dinv[0:8, :],
                    start=True, stop=True))(),
                    reads=("dinv", "e2h"), writes=pkeys(6))
                b.op("ve", (lambda ic=ic: lambda e: e.tensor_mul(
                    attT[:, ic, :], numer[:, ic, :], bank(6)))(),
                    reads=("numer",) + pkeys(6), writes=("attT",))
            for ic in range(IC):
                units.append((lambda ic=ic: lambda: u_div(ic))())

            def u_out(dc):
                py = 6 + (dc % 2)
                b.op("te", (lambda dc=dc, py=py: lambda e: e.matmul(
                    bank(py), wo_b[:, 0, dc * P:(dc + 1) * P], attT[:, 0, :],
                    start=True, stop=False))(),
                    reads=("wo_b", "attT"), writes=pkeys(py))
                for ic in range(1, IC):
                    b.op("te", (lambda dc=dc, ic=ic, py=py: lambda e: e.matmul(
                        bank(py), wo_b[:, ic, dc * P:(dc + 1) * P],
                        attT[:, ic, :],
                        start=False, stop=(ic == IC - 1)))(),
                        reads=("wo_b", "attT"), writes=pkeys(py))
                b.op("ve", (lambda dc=dc, py=py: lambda e: e.tensor_copy(
                    ybf[:, dc, :], bank(py)))(),
                    reads=pkeys(py), writes=("ybf",))
            for dc in range(DC):
                units.append((lambda dc=dc: lambda: u_out(dc))())

            def u_stats2():
                for dc in range(DC):
                    bf = dc % 2
                    if mm_g:
                        b.op("act", (lambda dc=dc, bf=bf: lambda e: e.activation(
                            yqs[:, bf, :], ybf[:, dc, :], AF.Square))(),
                            reads=("ybf",), writes=(f"yqs{bf}",))
                    else:
                        b.op("ve", (lambda dc=dc, bf=bf: lambda e: e.tensor_mul(
                            yqs[:, bf, :], ybf[:, dc, :], ybf[:, dc, :]))(),
                            reads=("ybf",), writes=(f"yqs{bf}",))
                    b.op("te", (lambda dc=dc: lambda e: e.matmul(
                        bank(6, rows=1), onesbf[:], ybf[:, dc, :],
                        start=(dc == 0), stop=(dc == DC - 1)))(),
                        reads=("ybf", "onesbf"), writes=pkeys(6))
                    b.op("te", (lambda dc=dc, bf=bf: lambda e: e.matmul(
                        bank(7, rows=1), onesbf[:], yqs[:, bf, :],
                        start=(dc == 0), stop=(dc == DC - 1)))(),
                        reads=(f"yqs{bf}", "onesbf"), writes=pkeys(7))
                b.op("ve", _lp(lambda e: e.tensor_copy(stage[0:1, 0:QTW], bank(6, rows=1))),
                     reads=pkeys(6), writes=("stage",))
                b.op("ve", _lp(lambda e: e.tensor_copy(stage[0:1, QTW:2 * QTW], bank(7, rows=1))),
                     reads=pkeys(7), writes=("stage",))
            units.append(u_stats2)

            def u_coef():
                emit_gather(1, crS[0:1, :], "crS", col0=0, mm=mm_g, pb=6)
                emit_gather(1, crQ[0:1, :], "crQ", col0=1, mm=mm_g, pb=7)
                b.op("ve", lambda e: e.tensor_scalar_mul(crS[0:1, :], crS[0:1, :], 1.0 / D),
                     reads=("crS",), writes=("crS",))
                b.op("ve", lambda e: e.tensor_scalar_mul(crQ[0:1, :], crQ[0:1, :], 1.0 / D),
                     reads=("crQ",), writes=("crQ",))
                b.op("ve", lambda e: e.tensor_mul(crA[0:1, :], crS[0:1, :], crS[0:1, :]),
                     reads=("crS",), writes=("crA",))
                b.op("ve", lambda e: e.tensor_sub(crQ[0:1, :], crQ[0:1, :], crA[0:1, :]),
                     reads=("crA", "crQ"), writes=("crQ",))
                b.op("act", lambda e: e.activation(
                    crA[0:1, :], crQ[0:1, :], AF.Sqrt, bias=consts[0:1, 1:2]),
                    reads=("crQ", "consts"), writes=("crA",))
                b.op("ve", lambda e: e.reciprocal(crA[0:1, :], crA[0:1, :]),
                     reads=("crA",), writes=("crA",))
                b.op("ve", lambda e: e.tensor_mul(crS[0:1, :], crS[0:1, :], crA[0:1, :]),
                     reads=("crA", "crS"), writes=("crS",))
                b.op("ve", _lp(lambda e: e.tensor_copy(coefx[32:33, :], crA[0:1, :])),
                     reads=("crA",), writes=("coefx",))
                b.op("ve", _lp(lambda e: e.tensor_copy(coefx[0:1, :], crS[0:1, :])),
                     reads=("crS",), writes=("coefx",))
                b.op("te", lambda e: e.matmul(bank(6), e36[:, 4 * P:5 * P], coefx[0:36, :],
                                              start=True, stop=True),
                     reads=("coefx", "e36"), writes=pkeys(6))
                b.op("te", lambda e: e.matmul(bank(7), e36[:, 0:P], coefx[0:36, :],
                                              start=True, stop=True),
                     reads=("coefx", "e36"), writes=pkeys(7))
                b.op("act", lambda e: e.activation(bsb[:, 0, :], bank(6), AF.Copy),
                     reads=pkeys(6), writes=("bsb0",))
                b.op("act", lambda e: e.activation(bsb[:, 1, :], bank(7), AF.Copy),
                     reads=pkeys(7), writes=("bsb1",))
            units.append(u_coef)

            def u_apply(dcs):
                for dc in dcs:
                    b.op("ve", (lambda dc=dc: lambda e: e.tensor_mul(
                        t1b[:], ybf[:, dc, :], bsb[:, 0, :]))(),
                        reads=("ybf", "bsb0"), writes=("t1",))
                    b.op("ve", (lambda dc=dc: lambda e: e.tensor_sub(
                        t1b[:], t1b[:], bsb[:, 1, :]))(),
                        reads=("t1", "bsb1"), writes=("t1",))
                    b.op("ve", (lambda dc=dc: lambda e: e.scalar_tensor_tensor(
                        xT[:, dc, tsl], t1b[:], g_out[:, dc:dc + 1], xT[:, dc, tsl],
                        ALU.mult, ALU.add))(),
                        reads=("t1", "g_out", f"xT{qt}"), writes=(f"xT{qt}",))
            units.append(lambda: u_apply(range(0, 4)))
            units.append(lambda: u_apply(range(4, 8)))
            if emit_post_stats:
                def u_stats():
                    tsl2 = slice(qt * QTW, (qt + 1) * QTW)
                    for dc in range(DC):
                        b.op("ve", (lambda dc=dc, tsl2=tsl2: lambda e: e.tensor_copy(
                            ybf[:, dc, :], xT[:, dc, tsl2]))(),
                            reads=(f"xT{qt}",), writes=("ybf",))
                    for dc in range(DC):
                        b.op("te", (lambda dc=dc: lambda e: e.matmul(
                            bank(6, rows=1), onesbf[:], ybf[:, dc, :],
                            start=(dc == 0), stop=(dc == DC - 1)))(),
                            reads=("ybf", "onesbf"), writes=pkeys(6))
                    b.op("ve", lambda e: e.tensor_copy(crB[0:1, :], bank(6, rows=1)),
                         reads=pkeys(6), writes=("crB",))
                    b.dma("sp", "ldg", (lambda qt=qt: lambda e: e.dma_start(
                        out=stg_d[(8 + qt) * QTW:(9 + qt) * QTW], in_=crB[0:1, :]))(),
                        reads=("crB",), writes=("stg_dS",))
                    for dc in range(DC):
                        b.op("ve", (lambda dc=dc: lambda e: e.tensor_mul(
                            ybf[:, dc, :], ybf[:, dc, :], ybf[:, dc, :]))(),
                            reads=("ybf",), writes=("ybf",))
                    for dc in range(DC):
                        b.op("te", (lambda dc=dc: lambda e: e.matmul(
                            bank(7, rows=1), onesbf[:], ybf[:, dc, :],
                            start=(dc == 0), stop=(dc == DC - 1)))(),
                            reads=("ybf", "onesbf"), writes=pkeys(7))
                    b.op("ve", lambda e: e.tensor_copy(crB[0:1, :], bank(7, rows=1)),
                         reads=pkeys(7), writes=("crB",))
                    b.dma("sp", "ldg", (lambda qt=qt: lambda e: e.dma_start(
                        out=stg_d[(12 + qt) * QTW:(13 + qt) * QTW], in_=crB[0:1, :]))(),
                        reads=("crB",), writes=("stg_dQ",))
                units.append(u_stats)
            return units

        pending = []
        for qt in range(NQT):
            tsl = slice(qt * QTW, (qt + 1) * QTW)
            batch = 0 if qt < 2 else 1
            chunks = list(range(batch * 64, batch * 64 + 64))
            for ic in range(IC):
                DEPTH = 2
                n = len(chunks)
                for j in range(n + DEPTH):
                    if j % 4 == 2 and pending:
                        pending.pop(0)()
                    if j < n:
                        c = chunks[j]
                        par3 = j % 3
                        sk = 2 * (j % 2)
                        b.op("te", (lambda c=c, ic=ic, tsl=tsl, sk=sk, batch=batch: lambda e: e.matmul(
                            bank(sk), kT[0:64, c * P:(c + 1) * P],
                            qT[0:64, ic, tsl], start=True, stop=True,
                            tile_position=(0, 0)))(),
                            reads=(f"kT{batch}", "qT"), writes=pkeys(sk))
                        b.op("te", (lambda c=c, ic=ic, tsl=tsl, sk=sk, batch=batch: lambda e: e.matmul(
                            bank(sk + 1), kT[64:128, c * P:(c + 1) * P],
                            qT[64:128, ic, tsl], start=True, stop=True,
                            tile_position=(64, 0)))(),
                            reads=(f"kT{batch}", "qT"), writes=pkeys(sk + 1))
                        b.op("act", (lambda par3=par3, sk=sk: lambda e: e.activation(
                            exp_sb[:, par3, :], bank(sk, nbanks=2), AF.Exp,
                            bias=consts[:, 0:1]))(),
                            reads=pkeys(sk, 2) + ("consts",), writes=(f"exp{par3}",))
                    if j >= DEPTH:
                        jj = j - DEPTH
                        cc = chunks[jj]
                        par2 = jj % 3
                        b.op("te", (lambda cc=cc, par2=par2, jj=jj, n=n, batch=batch: lambda e: e.matmul(
                            bank(4, rows=65), vaug[:, cc * 65:(cc + 1) * 65],
                            exp_sb[:, par2, 0:512],
                            start=(jj == 0), stop=(jj == n - 1)))(),
                            reads=(f"vaug{batch}", f"exp{par2}"), writes=pkeys(4))
                        b.op("te", (lambda cc=cc, par2=par2, jj=jj, n=n, batch=batch: lambda e: e.matmul(
                            bank(5, rows=65), vaug[:, cc * 65:(cc + 1) * 65],
                            exp_sb[:, par2, 512:1024],
                            start=(jj == 0), stop=(jj == n - 1)))(),
                            reads=(f"vaug{batch}", f"exp{par2}"), writes=pkeys(5))
                b.op("ve", (lambda ic=ic: lambda e: e.tensor_copy(
                    numer[0:64, ic, :], bank(4, rows=65)[0:64, :]))(),
                    reads=pkeys(4), writes=("numer",))
                b.op("ve", (lambda ic=ic: lambda e: e.tensor_copy(
                    numer[64:128, ic, :], bank(5, rows=65)[0:64, :]))(),
                    reads=pkeys(5), writes=("numer",))
                b.op("ve", (lambda ic=ic: _lp(lambda e: e.tensor_copy(
                    stage[0:1, (2 * ic) * QTW:(2 * ic + 1) * QTW],
                    bank(4, rows=65)[64:65, :])))(),
                    reads=pkeys(4), writes=("stage",))
                b.op("ve", (lambda ic=ic: _lp(lambda e: e.tensor_copy(
                    stage[0:1, (2 * ic + 1) * QTW:(2 * ic + 2) * QTW],
                    bank(5, rows=65)[64:65, :])))(),
                    reads=pkeys(5), writes=("stage",))
            for u in pending:
                u()
            pending = make_tail_units(qt)
        for u in pending:
            u()
        actx.close()
        b.barrier()

    _phases = int(os.environ.get("MK_PHASES", "3"))
    _hook_stats = os.environ.get("MK_HOOK", "1") == "1" and _phases >= 2
    if _phases >= 1:
        hk = None
        if _hook_stats:
            def hk(qt):
                emit_stats_qt(qt, _ffn_xb[0], pb=(6, 7))
                if qt == NQT - 1:
                    emit_coef_block(attn_double=True)
        emit_ffn(0, post_hook=hk)
    if _phases >= 2:
        emit_attn(sub=int(os.environ.get("MK_ATTN_SUB", "3")),
                  stats_done=_hook_stats, emit_post_stats=(_phases >= 3 and os.environ.get("MK_PSTATS", "0") == "1"))
    if _phases >= 3:
        def store_qt(qt):
            tsl = slice(qt * QTW, (qt + 1) * QTW)
            b.dma("po", "pod", (lambda qt=qt, tsl=tsl: lambda e: e.dma_start(
                out=out_d.rearrange("p (dc t) -> p dc t", t=T)[:, :, tsl],
                in_=xT[:, :, tsl]))(),
                reads=(f"xT{qt}",), writes=("out",))
        emit_ffn(1, post_hook=store_qt,
                 skip_stats=(os.environ.get("MK_PSTATS", "0") == "1"))

    if _phases < 3:
        for i in range(4):
            b.dma("po", "pod", (lambda i=i: lambda e: e.dma_start(
                out=out_d[:, i * 4096:(i + 1) * 4096],
                in_=xT.ap().rearrange("p dc t -> p (dc t)")[:, i * 4096:(i + 1) * 4096]))(),
                reads=("xT0", "xT1", "xT2", "xT3"), writes=("out",))
    b.barrier()

    b.build(nc, ctx)
    ctx.close()
    return nc


_NC_CACHE = None


def kernel(L0_attn_x, L0_attn_norm_g, L0_attn_Wq, L0_attn_Wkv, L0_attn_Wout,
           L0_attn_out_norm_g, L0_ffn_w_gate, L0_ffn_w_up, L0_ffn_w_down):
    global _NC_CACHE
    x = np.asarray(L0_attn_x, dtype=np.float32)
    norm_g = np.asarray(L0_attn_norm_g, dtype=np.float32)
    out_g = np.asarray(L0_attn_out_norm_g, dtype=np.float32)
    Wq = np.asarray(L0_attn_Wq, dtype=np.float32)
    Wkv = np.asarray(L0_attn_Wkv, dtype=np.float32)
    Wout = np.asarray(L0_attn_Wout, dtype=np.float32)
    wg = np.asarray(L0_ffn_w_gate, dtype=np.int32)
    wu = np.asarray(L0_ffn_w_up, dtype=np.int32)
    wd = np.asarray(L0_ffn_w_down, dtype=np.int32)

    wg_t = np.ascontiguousarray(wg.T.reshape(D, NHC, P).transpose(1, 0, 2))
    wu_t = np.ascontiguousarray(wu.T.reshape(D, NHC, P).transpose(1, 0, 2))
    wd_t = np.ascontiguousarray(wd.T.reshape(NHC, P, D))
    wq_t = np.ascontiguousarray(Wq.reshape(DC, P, 512))
    wkv_t = np.ascontiguousarray(Wkv.reshape(DC, P, P))
    wo_t = np.ascontiguousarray(Wout.reshape(IC, P, D))
    gatt = np.ascontiguousarray(norm_g.reshape(DC, P).T)
    gout = np.ascontiguousarray(out_g.reshape(DC, P).T)

    if _NC_CACHE is None:
        _NC_CACHE = build_program()
    nc = _NC_CACHE

    in_maps = []
    for c in range(8):
        sl = slice(c * 1024, (c + 1) * 1024)
        xc = np.concatenate([x[0, sl], x[1, sl]], axis=0)
        xc_fm = np.ascontiguousarray(xc.T)
        in_maps.append({
            "x": xc_fm, "wg": wg_t, "wu": wu_t, "wd": wd_t,
            "wq": wq_t, "wkv": wkv_t, "wo": wo_t,
            "gatt": gatt, "gout": gout,
        })
    res = run_bass_kernel_spmd(nc, in_maps, core_ids=list(range(8)))
    out = np.empty((2, 8192, D), dtype=np.float32)
    for c in range(8):
        o = np.asarray(res.results[c]["out"], dtype=np.float32)
        o = o.reshape(P, DC, T).transpose(2, 1, 0).reshape(T, D)
        sl = slice(c * 1024, (c + 1) * 1024)
        out[0, sl] = o[0:1024]
        out[1, sl] = o[1024:2048]
    return out
